# version 02
# baseline (speedup 1.0000x reference)
import numpy as np
import ml_dtypes
from contextlib import ExitStack

import concourse.bass as bass
import concourse.mybir as mybir
from concourse.bass_utils import run_bass_kernel_spmd

F32 = mybir.dt.float32
BF16 = mybir.dt.bfloat16
AF = mybir.ActivationFunctionType
ALU = mybir.AluOpType
AX = mybir.AxisListType

NCORE = 8
S = 8192
D = 4096
TOK = 1024
NT = TOK // 128
NCH = D // 128
QSEL = 1024
E0 = 64
HKV = 8
NQH = 32
CD = 512
NG = 16

npbf = ml_dtypes.bfloat16


def ts(i, n):
    return slice(i * n, (i + 1) * n)


class Ctx:

    def __init__(self, nc, stack):
        self.nc = nc
        self.h = {}
        self.v = {}
        for n in ["ld", "wld", "mm", "vv", "aa", "gg", "out", "cc", "ch", "ach"]:
            self.h[n] = stack.enter_context(nc.semaphore(n))
            self.v[n] = 0

    def inc(self, instr, name, amt=16):
        instr.then_inc(self.h[name], amt)
        self.v[name] += amt

    def wait(self, eng, name, val=None):
        eng.wait_ge(self.h[name], self.v[name] if val is None else val)

    def chain(self, eng, instr, name="ch"):
        instr.then_inc(self.h[name], 1)
        self.v[name] += 1
        eng.wait_ge(self.h[name], self.v[name])



def build_phaseA(nc, io):
    st = ExitStack()
    with st:
        e = st.enter_context
        cx = Ctx(nc, st)

        ident = e(nc.sbuf_tensor("ident", [128, 128], BF16))
        epsl = e(nc.sbuf_tensor("epsl", [128, 1], F32))
        identf = e(nc.sbuf_tensor("identf", [128, 128], F32))
        rw_bf = e(nc.sbuf_tensor("rw_bf", [128, NCH, E0], BF16))
        rbias_bc = e(nc.sbuf_tensor("rbias_bc", [128, E0], F32))
        eb_bf = e(nc.sbuf_tensor("eb_bf", [64, D], BF16))
        m_bf = e(nc.sbuf_tensor("m_bf", [128, HKV, 128], BF16))
        z_bf = e(nc.sbuf_tensor("z_bf", [128, HKV], BF16))
        bqs = e(nc.sbuf_tensor("bqs", [128, HKV], F32))
        xn_fm = e(nc.sbuf_tensor("xn_fm", [128, NCH, TOK], BF16))
        wsc = e(nc.sbuf_tensor("wsc", [128, NT], F32))
        wbi = e(nc.sbuf_tensor("wbi", [128, NT], F32))
        r0t = e(nc.sbuf_tensor("r0t", [128, NT], F32))
        nmrt = e(nc.sbuf_tensor("nmrt", [128, NT], F32))
        wdT = e(nc.sbuf_tensor("wdT", [64, NT, 128], BF16))
        retr = e(nc.sbuf_tensor("retr", [128, HKV, TOK], BF16))
        partials = e(nc.sbuf_tensor("partials_sb", [128, NCH], F32))

        with nc.Block() as block:
            @block.sync
            def _(sync):
                cx.inc(sync.dma_start(out=identf[:, :], in_=io["idmat"]), "ld", 16)
                cx.inc(sync.dma_start(out=rbias_bc[:, :],
                       in_=bass.AP(io["rbias"].tensor, 0, [[0, 128], [1, E0]])), "ld", 16)
                cx.inc(sync.dma_start(out=bqs[:, :], in_=io["bqsel_pc"]), "ld", 16)
                cx.wait(sync, "ld")

            @block.gpsimd
            def _(gpsimd):
                gpsimd.memset(epsl[:, :], 1e-5)
                cx.inc(gpsimd.dma_start(out=rw_bf[:, :, :],
                       in_=io["rwT"].rearrange("(k p) e -> p k e", p=128)), "wld", 16)
                cx.inc(gpsimd.dma_start(out=eb_bf[:, :], in_=io["eb"]), "wld", 16)
                cx.inc(gpsimd.dma_start(out=m_bf[:, :, :],
                       in_=io["m_pc"].rearrange("p (h e) -> p h e", h=HKV)), "wld", 16)
                cx.inc(gpsimd.dma_start(out=z_bf[:, :], in_=io["z_pc"]), "wld", 16)
                cx.wait(gpsimd, "wld")

        with nc.Block() as block:
            @block.vector
            def _(vector):
                vector.tensor_copy(ident[:, :], identf[:, :])

        xnb = e(nc.sbuf_tensor("xnb", [128, D], BF16))
        bn6 = e(nc.sbuf_tensor("bn6", [128, 48], F32))
        bn2 = e(nc.sbuf_tensor("bn2", [128, 2], F32))
        sq = e(nc.sbuf_tensor("sq", [128, 1], F32))
        lg = e(nc.sbuf_tensor("lg", [128, E0], F32))
        nm8 = e(nc.sbuf_tensor("nm8", [128, 1], F32))
        ee = e(nc.sbuf_tensor("ee", [128, E0], F32))
        sall = e(nc.sbuf_tensor("sall", [128, 1], F32))
        t8v = e(nc.sbuf_tensor("t8v", [128, 8], F32))
        wd = e(nc.sbuf_tensor("wd", [128, E0], F32))
        s8 = e(nc.sbuf_tensor("s8", [128, 1], F32))
        rsall = e(nc.sbuf_tensor("rsall", [128, 1], F32))
        tmp1 = e(nc.sbuf_tensor("tmp1", [128, 1], F32))
        wdnb = e(nc.sbuf_tensor("wdnb", [128, E0], BF16))
        stxt = ExitStack()
        xt = [stxt.enter_context(nc.sbuf_tensor(f"xt{b}", [128, D], F32)) for b in range(2)]

        with nc.Block() as block:
            @block.sync
            def _(sync):
                ii = sync.dma_start(out=xt[0][:, :], in_=io["x"][ts(0, 128), :])
                cx.inc(ii, "ld", 16)

        with (
            nc.psum_tensor("ptA0", [128, 128], BF16) as pt0,
            nc.psum_tensor("ptA1", [128, 128], BF16) as pt1,
            nc.psum_tensor("ptA2", [128, 128], BF16) as pt2,
            nc.psum_tensor("prA", [128, E0], F32) as psr,
            nc.psum_tensor("pwdT", [64, 128], BF16) as pwdT,
        ):
            ptl = [pt0, pt1, pt2]
            for i in range(NT):
                b = i % 2
                ld_tile = 48 + 16 * (i + 1)
                with nc.Block() as block:
                    @block.sync
                    def _(sync, i=i, b=b, ld_tile=ld_tile):
                        sync.wait_ge(cx.h["ld"], ld_tile)
                        if i + 1 < NT:
                            ii = sync.dma_start(out=xt[1 - b][:, :],
                                                in_=io["x"][ts(i + 1, 128), :])
                            cx.inc(ii, "ld", 16)

                    @block.vector
                    def _(vector, i=i, b=b, ld_tile=ld_tile):
                        vector.wait_ge(cx.h["ld"], ld_tile)
                        for p in range(8):
                            vector.bn_stats(bn6[:, p * 6:(p + 1) * 6], xt[b][:, ts(p, 512)])
                with nc.Block() as block:
                    @block.vector
                    def _(vector):
                        vector.bn_aggr(bn2[:, :], bn6[:, :])
                with nc.Block() as block:
                    @block.scalar
                    def _(scalar):
                        scalar.activation(sq[:, :], bn2[:, 1:2], AF.Sqrt, bias=epsl[:, :], scale=1.0)
                with nc.Block() as block:
                    @block.vector
                    def _(vector, i=i):
                        cx.chain(vector, vector.reciprocal(r0t[:, i:i + 1], sq[:, :]))
                        cx.chain(vector, vector.tensor_tensor(tmp1[:, :], bn2[:, 0:1],
                                                              r0t[:, i:i + 1], op=ALU.mult))
                        vector.tensor_scalar(nmrt[:, i:i + 1], tmp1[:, :], -1.0, None, op0=ALU.mult)
                with nc.Block() as block:
                    @block.scalar
                    def _(scalar, i=i, b=b):
                        scalar.activation(xnb[:, :], xt[b][:, :], AF.Identity,
                                          bias=nmrt[:, i:i + 1], scale=r0t[:, i:i + 1])
                with nc.Block() as block:
                    mm0 = cx.v["mm"]
                    aa0 = cx.v["aa"]

                    @block.tensor
                    def _(tensor, i=i, mm0=mm0, aa0=aa0):
                        if i == 0:
                            cx.wait(tensor, "wld")
                        for k in range(NCH):
                            if k >= 3:
                                tensor.wait_ge(cx.h["aa"], aa0 + (k - 2))
                            tensor.transpose(ptl[k % 3][:, :], xnb[:, ts(k, 128)],
                                             ident[:, :]).then_inc(cx.h["mm"], 1)
                        for k in range(NCH):
                            tensor.wait_ge(cx.h["aa"], aa0 + (k + 1))
                            ii = tensor.matmul(psr[:, :], xn_fm[:, k, ts(i, 128)],
                                               rw_bf[:, k, :],
                                               start=(k == 0), stop=(k == NCH - 1))
                            if k == NCH - 1:
                                ii.then_inc(cx.h["mm"], 1)
                    cx.v["mm"] += NCH + 1

                    @block.scalar
                    def _(scalar, i=i, mm0=mm0):
                        for k in range(NCH):
                            scalar.wait_ge(cx.h["mm"], mm0 + (k + 1))
                            scalar.copy(xn_fm[:, k, ts(i, 128)],
                                        ptl[k % 3][:, :]).then_inc(cx.h["aa"], 1)
                    cx.v["aa"] += NCH

                    @block.vector
                    def _(vector, mm0=mm0):
                        vector.wait_ge(cx.h["mm"], mm0 + NCH + 1)
                        cx.chain(vector, vector.tensor_tensor(lg[:, :], psr[:, :], rbias_bc[:, :],
                                                              op=ALU.add))
                        cx.chain(vector, vector.reduce_max(tmp1[:, :], lg[:, :], axis=AX.X))
                        vector.tensor_scalar(nm8[:, :], tmp1[:, :], -1.0, None, op0=ALU.mult)
                with nc.Block() as block:
                    @block.scalar
                    def _(scalar):
                        scalar.activation(ee[:, :], lg[:, :], AF.Exp, bias=nm8[:, :],
                                          scale=1.0, accum_out=sall[:, :])
                with nc.Block() as block:
                    @block.vector
                    def _(vector):
                        vector.max(t8v[:, :], ee[:, :])
                with nc.Block() as block:
                    @block.vector
                    def _(vector):
                        vector.scalar_tensor_tensor(wd[:, :], ee[:, :], t8v[:, 7:8], ee[:, :],
                                                    op0=ALU.is_ge, op1=ALU.mult,
                                                    accum_out=s8[:, :])
                with nc.Block() as block:
                    @block.vector
                    def _(vector, i=i):
                        cx.chain(vector, vector.reciprocal(rsall[:, :], sall[:, :]))
                        cx.chain(vector, vector.tensor_tensor(tmp1[:, :], s8[:, :], rsall[:, :],
                                                              op=ALU.mult))
                        vector.tensor_tensor(wsc[:, i:i + 1], tmp1[:, :], r0t[:, i:i + 1],
                                             op=ALU.mult)
                        vector.tensor_tensor(wbi[:, i:i + 1], tmp1[:, :], nmrt[:, i:i + 1],
                                             op=ALU.mult)
                        vector.tensor_scalar(wdnb[:, :], wd[:, :], rsall[:, :], None, op0=ALU.mult)
                with nc.Block() as block:
                    @block.tensor
                    def _(tensor):
                        tensor.transpose(pwdT[:, :], wdnb[:, :], ident[:, :])
                with nc.Block() as block:
                    @block.scalar
                    def _(scalar, i=i):
                        scalar.copy(wdT[:, i, :], pwdT[:, :])

        stxt.close()

        st2 = ExitStack()
        wqsel_bf = st2.enter_context(nc.sbuf_tensor("wqsel_bf", [128, NCH, QSEL], BF16))
        sig = st2.enter_context(nc.sbuf_tensor("sig", [128, HKV, TOK], BF16))
        e1 = st2.enter_context(nc.sbuf_tensor("e1", [128, 512], F32))
        e2 = st2.enter_context(nc.sbuf_tensor("e2", [128, 512], F32))
        m1 = st2.enter_context(nc.sbuf_tensor("m1", [128, 512], F32))
        with nc.Block() as block:
            @block.gpsimd
            def _(gpsimd):
                ii = gpsimd.dma_start(out=wqsel_bf[:, :, :],
                                      in_=io["wqselT"].rearrange("(k p) q -> p k q", p=128))
                cx.inc(ii, "wld", 16)
        with ExitStack() as ps_st:
            psq = [ps_st.enter_context(nc.psum_tensor(f"psq{a}", [128, 512], F32))
                   for a in range(HKV)]
            with nc.Block() as block:
                mm0 = cx.v["mm"]
                vv0 = cx.v["vv"]
                aa0 = cx.v["aa"]

                @block.tensor
                def _(tensor, mm0=mm0, vv0=vv0):
                    cx.wait(tensor, "wld")
                    for th in range(2):
                        if th == 1:
                            tensor.wait_ge(cx.h["vv"], vv0 + 16)
                        for k in range(NCH):
                            for a in range(HKV):
                                ii = tensor.matmul(psq[a][:, :], wqsel_bf[:, k, ts(a, 128)],
                                                   xn_fm[:, k, ts(th, 512)],
                                                   start=(k == 0), stop=(k == NCH - 1))
                                if k == NCH - 1 and a == HKV - 1:
                                    ii.then_inc(cx.h["mm"], 1)
                cx.v["mm"] += 2

                @block.vector
                def _(vector, mm0=mm0, aa0=aa0, vv0=vv0):
                    for th in range(2):
                        vector.wait_ge(cx.h["mm"], mm0 + th + 1)
                        for a in range(HKV):
                            idx = th * HKV + a
                            if idx >= 1:
                                vector.wait_ge(cx.h["aa"], aa0 + idx)
                            vector.tensor_scalar(e1[:, :], psq[a][:, :], bqs[:, a:a + 1], 0.0,
                                                 op0=ALU.add, op1=ALU.min)
                            vector.tensor_scalar(m1[:, :], psq[a][:, :], bqs[:, a:a + 1], 0.0,
                                                 op0=ALU.add, op1=ALU.max).then_inc(cx.h["vv"], 1)
                            vector.wait_ge(cx.h["aa"], aa0 + idx + 1)
                            vector.tensor_tensor(sig[:, a, ts(th, 512)], e2[:, :], m1[:, :],
                                                 op=ALU.add).then_inc(cx.h["vv"], 1)
                cx.v["vv"] += 4 * HKV

                @block.scalar
                def _(scalar, vv0=vv0):
                    for th in range(2):
                        for a in range(HKV):
                            idx = th * HKV + a
                            scalar.wait_ge(cx.h["vv"], vv0 + 2 * idx + 1)
                            scalar.activation(e2[:, :], e1[:, :], AF.Exp,
                                              bias=0.0, scale=1.0).then_inc(cx.h["aa"], 1)
                cx.v["aa"] += 2 * HKV

        denf = st2.enter_context(nc.sbuf_tensor("denf", [1, TOK], F32))
        denr = st2.enter_context(nc.sbuf_tensor("denr", [1, TOK], F32))
        denb = st2.enter_context(nc.sbuf_tensor("denb", [128, TOK], F32))
        with (
            nc.psum_tensor("pnum", [128, TOK], F32) as pnum,
            nc.psum_tensor("pden", [1, TOK], F32) as pden,
        ):
            for a in range(HKV):
                with nc.Block() as block:
                    @block.tensor
                    def _(tensor, a=a):
                        for th in range(2):
                            tensor.matmul(pnum[:, ts(th, 512)], m_bf[:, a, :],
                                          sig[:, a, ts(th, 512)], start=True, stop=True)
                            tensor.matmul(pden[:, ts(th, 512)], z_bf[:, a:a + 1],
                                          sig[:, a, ts(th, 512)], start=True, stop=True)
                with nc.Block() as block:
                    @block.vector
                    def _(vector):
                        vector.tensor_scalar(denf[:, :], pden[:, :], 1e-6, None, op0=ALU.add)
                with nc.Block() as block:
                    @block.vector
                    def _(vector):
                        vector.reciprocal(denr[:, :], denf[:, :])
                with nc.Block() as block:
                    @block.sync
                    def _(sync):
                        ii = sync.dma_start(out=io["denb_d"], in_=denr[:, :])
                        cx.inc(ii, "out", 16)
                        cx.wait(sync, "out")
                        ii = sync.dma_start(out=denb[:, :],
                                            in_=bass.AP(io["denb_d"].tensor, 0,
                                                        [[0, 128], [1, TOK]]))
                        cx.inc(ii, "ld", 16)
                        cx.wait(sync, "ld")
                with nc.Block() as block:
                    @block.vector
                    def _(vector, a=a):
                        vector.tensor_tensor(retr[:, a, :], pnum[:, :], denb[:, :], op=ALU.mult)

        st2.close()

        st4 = ExitStack()
        e4 = st4.enter_context
        wo_buf = [e4(nc.sbuf_tensor(f"wo_buf{b}", [128, NCH, 512], BF16)) for b in range(2)]
        xsl = [e4(nc.sbuf_tensor(f"xsl{b}", [128, 512], F32)) for b in range(2)]
        scs = [e4(nc.sbuf_tensor(f"scs{b}", [128, 512], F32)) for b in range(2)]
        t1s = e4(nc.sbuf_tensor("t1s", [128, 512], F32))
        x1sl = [e4(nc.sbuf_tensor(f"x1sl{b}", [128, 512], F32)) for b in range(2)]

        M0 = cx.v["mm"]; V0 = cx.v["vv"]; A0 = cx.v["aa"]
        L0 = cx.v["ld"]; O0 = cx.v["out"]; W0 = cx.v["wld"]
        with nc.Block() as block:
            @block.gpsimd
            def _(gpsimd):
                ii = gpsimd.dma_start(
                    out=wo_buf[0][:, :, :],
                    in_=bass.AP(io["woT0"].tensor, 0, [[D, 128], [128 * D, NCH], [1, 512]]))
                cx.inc(ii, "wld", 16)

            @block.sync
            def _(sync):
                ii = sync.dma_start(out=xsl[0][:, :], in_=io["x"][ts(0, 128), 0:512])
                cx.inc(ii, "ld", 16)

        with (
            nc.psum_tensor("po0", [128, 512], F32) as po0,
            nc.psum_tensor("po1", [128, 512], F32) as po1,
            nc.Block() as block,
        ):
            pol = [po0, po1]

            @block.gpsimd
            def _(gpsimd):
                for g in range(1, 8):
                    gpsimd.wait_ge(cx.h["mm"], M0 + 8 * (g - 1))
                    ii = gpsimd.dma_start(
                        out=wo_buf[g % 2][:, :, :],
                        in_=bass.AP(io["woT0"].tensor, g * 512,
                                    [[D, 128], [128 * D, NCH], [1, 512]]))
                    cx.inc(ii, "wld", 16)

            @block.tensor
            def _(tensor):
                for n in range(64):
                    g, i = divmod(n, 8)
                    if i == 0:
                        tensor.wait_ge(cx.h["wld"], W0 + 16 * (g + 1))
                    if n >= 2:
                        tensor.wait_ge(cx.h["vv"], V0 + 2 * n - 3)
                    pp = pol[n % 2]
                    for qh in range(NQH):
                        tensor.matmul(pp[:, :], retr[:, qh // 4, ts(i, 128)],
                                      wo_buf[g % 2][:, qh, :],
                                      start=(qh == 0), stop=False)
                    tensor.matmul(pp[:, :], wdT[:, i, :], eb_bf[:, ts(g, 512)],
                                  start=False, stop=True).then_inc(cx.h["mm"], 1)
            cx.v["mm"] += 64

            @block.scalar
            def _(scalar):
                for n in range(64):
                    g, i = divmod(n, 8)
                    scalar.wait_ge(cx.h["ld"], L0 + 16 * (n + 1))
                    if n >= 2:
                        scalar.wait_ge(cx.h["vv"], V0 + 2 * n - 2)
                    scalar.activation(scs[n % 2][:, :], xsl[n % 2][:, :], AF.Identity,
                                      bias=wbi[:, i:i + 1],
                                      scale=wsc[:, i:i + 1]).then_inc(cx.h["aa"], 1)
            cx.v["aa"] += 64

            @block.vector
            def _(vector):
                for n in range(64):
                    vector.wait_ge(cx.h["mm"], M0 + n + 1)
                    vector.wait_ge(cx.h["ld"], L0 + 16 * (n + 1))
                    vector.tensor_tensor(t1s[:, :], pol[n % 2][:, :], xsl[n % 2][:, :],
                                         op=ALU.add).then_inc(cx.h["vv"], 1)
                    vector.wait_ge(cx.h["vv"], V0 + 2 * n + 1)
                    vector.wait_ge(cx.h["aa"], A0 + n + 1)
                    if n >= 2:
                        vector.wait_ge(cx.h["out"], O0 + 16 * (n - 1))
                    vector.tensor_tensor(x1sl[n % 2][:, :], t1s[:, :], scs[n % 2][:, :],
                                         op=ALU.add).then_inc(cx.h["vv"], 1)
            cx.v["vv"] += 128

            @block.sync
            def _(sync):
                for n in range(64):
                    if n + 1 < 64:
                        gg, ii2 = divmod(n + 1, 8)
                        if n >= 1:
                            sync.wait_ge(cx.h["vv"], V0 + 2 * n - 1)
                            sync.wait_ge(cx.h["aa"], A0 + n)
                        jj = sync.dma_start(out=xsl[(n + 1) % 2][:, :],
                                            in_=io["x"][ts(ii2, 128), ts(gg, 512)])
                        cx.inc(jj, "ld", 16)
                    g, i = divmod(n, 8)
                    sync.wait_ge(cx.h["vv"], V0 + 2 * n + 2)
                    jj = sync.dma_start(out=io["x1"][ts(i, 128), ts(g, 512)],
                                        in_=x1sl[n % 2][:, :])
                    cx.inc(jj, "out", 16)

        st4.close()
        x1t = [e(nc.sbuf_tensor(f"x1t{b}", [128, D], F32)) for b in range(2)]
        xn1b = e(nc.sbuf_tensor("xn1b", [128, D], BF16))
        with nc.Block() as block:
            @block.sync
            def _(sync):
                cx.wait(sync, "out")
                ii = sync.dma_start(out=x1t[0][:, :], in_=io["x1"][ts(0, 128), :])
                cx.inc(ii, "ld", 16)

        ld5 = cx.v["ld"]
        with (
            nc.psum_tensor("pt5a", [128, 128], BF16) as pt0b,
            nc.psum_tensor("pt5b", [128, 128], BF16) as pt1b,
            nc.psum_tensor("pt5c", [128, 128], BF16) as pt2b,
        ):
            ptl5 = [pt0b, pt1b, pt2b]
            for i in range(NT):
                b = i % 2
                ld_tile = ld5 + 16 * i
                with nc.Block() as block:
                    @block.sync
                    def _(sync, i=i, b=b, ld_tile=ld_tile):
                        sync.wait_ge(cx.h["ld"], ld_tile)
                        if i + 1 < NT:
                            ii = sync.dma_start(out=x1t[1 - b][:, :],
                                                in_=io["x1"][ts(i + 1, 128), :])
                            cx.inc(ii, "ld", 16)

                    @block.vector
                    def _(vector, b=b, ld_tile=ld_tile):
                        vector.wait_ge(cx.h["ld"], ld_tile)
                        for p in range(8):
                            vector.bn_stats(bn6[:, p * 6:(p + 1) * 6], x1t[b][:, ts(p, 512)])
                with nc.Block() as block:
                    @block.vector
                    def _(vector):
                        vector.bn_aggr(bn2[:, :], bn6[:, :])
                with nc.Block() as block:
                    @block.scalar
                    def _(scalar):
                        scalar.activation(sq[:, :], bn2[:, 1:2], AF.Sqrt, bias=epsl[:, :], scale=1.0)
                with nc.Block() as block:
                    @block.vector
                    def _(vector, i=i):
                        cx.chain(vector, vector.reciprocal(r0t[:, i:i + 1], sq[:, :]))
                        cx.chain(vector, vector.tensor_tensor(tmp1[:, :], bn2[:, 0:1],
                                                              r0t[:, i:i + 1], op=ALU.mult))
                        vector.tensor_scalar(nmrt[:, i:i + 1], tmp1[:, :], -1.0, None, op0=ALU.mult)
                with nc.Block() as block:
                    @block.scalar
                    def _(scalar, i=i, b=b):
                        scalar.activation(xn1b[:, :], x1t[b][:, :], AF.Identity,
                                          bias=nmrt[:, i:i + 1], scale=r0t[:, i:i + 1])
                with nc.Block() as block:
                    mm0 = cx.v["mm"]
                    aa0 = cx.v["aa"]

                    @block.tensor
                    def _(tensor, i=i, mm0=mm0, aa0=aa0):
                        for k in range(NCH):
                            if k >= 3:
                                tensor.wait_ge(cx.h["aa"], aa0 + (k - 2))
                            tensor.transpose(ptl5[k % 3][:, :], xn1b[:, ts(k, 128)],
                                             ident[:, :]).then_inc(cx.h["mm"], 1)
                    cx.v["mm"] += NCH

                    @block.scalar
                    def _(scalar, i=i, mm0=mm0):
                        for k in range(NCH):
                            scalar.wait_ge(cx.h["mm"], mm0 + (k + 1))
                            scalar.copy(xn_fm[:, k, ts(i, 128)],
                                        ptl5[k % 3][:, :]).then_inc(cx.h["aa"], 1)
                    cx.v["aa"] += NCH

                    @block.sync
                    def _(sync, i=i):
                        ii = sync.dma_start(out=io["xn1tm"][ts(i, 128), :], in_=xn1b[:, :])
                        cx.inc(ii, "out", 16)

        with nc.Block() as block:
            @block.vector
            def _(vector):
                for k in range(NCH):
                    vector.reduce_sum(partials[:, k:k + 1], xn_fm[:, k, :], axis=AX.X)
        with nc.Block() as block:
            @block.sync
            def _(sync):
                ii = sync.dma_start(out=io["xn1fm"], in_=xn_fm[:, :, :])
                cx.inc(ii, "out", 16)
                ii = sync.dma_start(out=io["partials"], in_=partials[:, :])
                cx.inc(ii, "out", 16)
                cx.wait(sync, "out")
    return nc



def build_phaseCD(nc, io):
    st = ExitStack()
    with st:
        e = st.enter_context
        cx = Ctx(nc, st)

        ident = e(nc.sbuf_tensor("identCD", [128, 128], BF16))
        identf = e(nc.sbuf_tensor("identfCD", [128, 128], F32))
        xn1fm = e(nc.sbuf_tensor("xn1fm_sb", [128, NCH, TOK], BF16))
        cvec = e(nc.sbuf_tensor("cvec", [128, 8], F32))
        onesc = e(nc.sbuf_tensor("onesc", [1, 128], BF16))

        with nc.Block() as block:
            @block.sync
            def _(sync):
                cx.inc(sync.dma_start(out=identf[:, :], in_=io["idmat"]), "ld", 16)
                cx.inc(sync.dma_start(out=xn1fm[:, :, :], in_=io["xn1fm"]), "ld", 16)
                cx.wait(sync, "ld")
        with nc.Block() as block:
            @block.vector
            def _(vector):
                vector.tensor_copy(ident[:, :], identf[:, :])
            @block.gpsimd
            def _(gpsimd):
                gpsimd.memset(onesc[:, :], 1.0)

        with ExitStack() as s1:
            pall = s1.enter_context(nc.sbuf_tensor("pall", [128, 8, NCH], F32))
            acc = [s1.enter_context(nc.sbuf_tensor(f"acc{b}", [128, NCH], F32)) for b in range(2)]
            mean_bf = s1.enter_context(nc.sbuf_tensor("mean_bf", [128, NCH], BF16))
            wql1_bf = s1.enter_context(nc.sbuf_tensor("wql1_bf", [128, NCH, 128], BF16))
            lk_bf = s1.enter_context(nc.sbuf_tensor("lk_bf", [128, 8], BF16))
            qcol_bf = s1.enter_context(nc.sbuf_tensor("qcol_bf", [128, 1], BF16))
            sc8 = s1.enter_context(nc.sbuf_tensor("sc8", [8, 1], F32))
            scr = s1.enter_context(nc.sbuf_tensor("scr", [1, 8], F32))
            mx1 = s1.enter_context(nc.sbuf_tensor("mx1", [1, 1], F32))
            nmx = s1.enter_context(nc.sbuf_tensor("nmx", [1, 1], F32))
            ee8 = s1.enter_context(nc.sbuf_tensor("ee8", [1, 8], F32))
            se8 = s1.enter_context(nc.sbuf_tensor("se8", [1, 1], F32))
            rc1 = s1.enter_context(nc.sbuf_tensor("rc1", [1, 1], F32))
            cf8 = s1.enter_context(nc.sbuf_tensor("cf8", [1, 8], F32))
            with nc.Block() as block:
                @block.sync
                def _(sync):
                    cx.inc(sync.dma_start(
                        out=pall[:, :, :],
                        in_=io["partials_all"].rearrange("r p k -> p r k")), "ld", 16)
                    cx.wait(sync, "ld")
                @block.gpsimd
                def _(gpsimd):
                    cx.inc(gpsimd.dma_start(
                        out=wql1_bf[:, :, :],
                        in_=io["wql1T"].rearrange("(k p) q -> p k q", p=128)), "wld", 16)
                    cx.inc(gpsimd.dma_start(out=lk_bf[:, :], in_=io["lkT"]), "wld", 16)
                    cx.wait(gpsimd, "wld")
            with nc.Block() as block:
                @block.vector
                def _(vector):
                    cx.chain(vector, vector.tensor_tensor(acc[0][:, :], pall[:, 0, :],
                                                          pall[:, 1, :], op=ALU.add))
                    for r in range(2, 8):
                        b = r % 2
                        cx.chain(vector, vector.tensor_tensor(acc[1 - b][:, :], acc[b][:, :],
                                                              pall[:, r, :], op=ALU.add))
            with nc.Block() as block:
                @block.scalar
                def _(scalar):
                    scalar.activation(mean_bf[:, :], acc[0][:, :], AF.Identity,
                                      bias=0.0, scale=1.0 / 8192.0)
            with (
                nc.psum_tensor("pql", [128, 1], F32) as pql,
                nc.psum_tensor("psc", [8, 1], F32) as psc,
            ):
                with nc.Block() as block:
                    @block.tensor
                    def _(tensor):
                        for k in range(NCH):
                            tensor.matmul(pql[:, :], wql1_bf[:, k, :], mean_bf[:, k:k + 1],
                                          start=(k == 0), stop=(k == NCH - 1))
                with nc.Block() as block:
                    @block.scalar
                    def _(scalar):
                        scalar.copy(qcol_bf[:, :], pql[:, :])
                with nc.Block() as block:
                    @block.tensor
                    def _(tensor):
                        tensor.matmul(psc[:, :], lk_bf[:, :], qcol_bf[:, :],
                                      start=True, stop=True)
                with nc.Block() as block:
                    @block.vector
                    def _(vector):
                        vector.tensor_scalar(sc8[:, :], psc[:, :], 1.0 / float(np.sqrt(128.0)),
                                             None, op0=ALU.mult)
            with nc.Block() as block:
                @block.sync
                def _(sync):
                    cx.inc(sync.dma_start(out=io["sc8_d"], in_=sc8[:, :]), "out", 16)
                    cx.wait(sync, "out")
                    cx.inc(sync.dma_start(out=scr[:, :],
                                          in_=bass.AP(io["sc8_d"].tensor, 0, [[1, 1], [1, 8]])),
                           "ld", 16)
                    cx.wait(sync, "ld")
            with nc.Block() as block:
                @block.vector
                def _(vector):
                    cx.chain(vector, vector.reduce_max(mx1[:, :], scr[:, :], axis=AX.X))
                    vector.tensor_scalar(nmx[:, :], mx1[:, :], -1.0, None, op0=ALU.mult)
            with nc.Block() as block:
                @block.scalar
                def _(scalar):
                    scalar.activation(ee8[:, :], scr[:, :], AF.Exp, bias=nmx[:, :],
                                      scale=1.0, accum_out=se8[:, :])
            with nc.Block() as block:
                @block.vector
                def _(vector):
                    cx.chain(vector, vector.reciprocal(rc1[:, :], se8[:, :]))
                    vector.tensor_scalar(cf8[:, :], ee8[:, :], rc1[:, :], None, op0=ALU.mult)
            with nc.Block() as block:
                @block.sync
                def _(sync):
                    cx.inc(sync.dma_start(out=io["cf8_d"], in_=cf8[:, :]), "out", 16)
                    cx.wait(sync, "out")
                    cx.inc(sync.dma_start(out=cvec[:, :],
                                          in_=bass.AP(io["cf8_d"].tensor, 0, [[0, 128], [1, 8]])),
                           "ld", 16)
                    cx.wait(sync, "ld")

        with ExitStack() as s2:
            ld2_bf = s2.enter_context(nc.sbuf_tensor("ld2_bf", [128, NCH, 512], BF16))
            scg = s2.enter_context(nc.sbuf_tensor("scg", [128, 1], F32))
            hsc = s2.enter_context(nc.sbuf_tensor("hsc", [128, 4, TOK], BF16))
            with nc.Block() as block:
                @block.gpsimd
                def _(gpsimd):
                    cx.inc(gpsimd.dma_start(
                        out=ld2_bf[:, :, :],
                        in_=io["ld2"].rearrange("(k p) r -> p k r", p=128)), "wld", 16)
                    cx.wait(gpsimd, "wld")
            with nc.psum_tensor("ph", [128, 512], F32) as ph:
                for g in range(4):
                    with nc.Block() as block:
                        @block.vector
                        def _(vector, g=g):
                            cx.chain(vector, vector.tensor_copy(scg[0:64, :],
                                                                cvec[0:64, 2 * g:2 * g + 1]))
                            vector.tensor_copy(scg[64:128, :], cvec[64:128, 2 * g + 1:2 * g + 2])
                    for th in range(2):
                        with nc.Block() as block:
                            @block.tensor
                            def _(tensor, g=g, th=th):
                                for k in range(NCH):
                                    tensor.matmul(ph[:, :], ld2_bf[:, k, ts(g, 128)],
                                                  xn1fm[:, k, ts(th, 512)],
                                                  start=(k == 0), stop=(k == NCH - 1))
                        with nc.Block() as block:
                            @block.scalar
                            def _(scalar, g=g, th=th):
                                scalar.activation(hsc[:, g, ts(th, 512)], ph[:, :], AF.Identity,
                                                  bias=0.0, scale=scg[:, :])
            with nc.Block() as block:
                @block.sync
                def _(sync):
                    cx.inc(sync.dma_start(out=io["hsc_d"], in_=hsc[:, :, :]), "out", 16)

        with ExitStack() as s3:
            e3 = s3.enter_context
            wdk_bf = e3(nc.sbuf_tensor("wdk_bf", [128, NCH, 512], BF16))
            wkr_bf = e3(nc.sbuf_tensor("wkr_bf", [128, NCH, 512], BF16))
            wupk_bf = e3(nc.sbuf_tensor("wupk_bf", [128, 4, 512], BF16))
            wupv_bf = e3(nc.sbuf_tensor("wupv_bf", [128, 4, TOK], BF16))
            kcos = e3(nc.sbuf_tensor("kcos_sb", [128, NT, 512], F32))
            ksin = e3(nc.sbuf_tensor("ksin_sb", [128, NT, 512], F32))
            ckv_bf = e3(nc.sbuf_tensor("ckv_bf", [128, 4, TOK], BF16))
            kr0f = e3(nc.sbuf_tensor("kr0f", [128, 512], F32))
            kswp = e3(nc.sbuf_tensor("kswp", [128, 512], F32))
            t1r = e3(nc.sbuf_tensor("t1r", [128, 512], F32))
            t2r = e3(nc.sbuf_tensor("t2r", [128, 512], F32))
            k8 = e3(nc.sbuf_tensor("k8", [128, TOK], BF16))
            vtm = e3(nc.sbuf_tensor("vtm", [128, TOK], BF16))
            kfmt = e3(nc.sbuf_tensor("kfmt", [128, HKV, 128], BF16))
            with nc.Block() as block:
                @block.gpsimd
                def _(gpsimd):
                    cx.inc(gpsimd.dma_start(
                        out=wdk_bf[:, :, :],
                        in_=io["wdkT"].rearrange("(k p) c -> p k c", p=128)), "wld", 16)
                    cx.inc(gpsimd.dma_start(
                        out=wkr_bf[:, :, :],
                        in_=io["wkrT"].rearrange("(k p) c -> p k c", p=128)), "wld", 16)
                    cx.inc(gpsimd.dma_start(
                        out=wupk_bf[:, :, :],
                        in_=io["wupkT"].rearrange("(k p) c -> p k c", p=128)), "wld", 16)
                    cx.inc(gpsimd.dma_start(
                        out=wupv_bf[:, :, :],
                        in_=io["wupvT"].rearrange("(k p) c -> p k c", p=128)), "wld", 16)
                    cx.wait(gpsimd, "wld")
                @block.sync
                def _(sync):
                    cx.inc(sync.dma_start(
                        out=kcos[:, :, :],
                        in_=io["kcosE"].rearrange("(i p) c -> p i c", p=128)), "ld", 16)
                    cx.inc(sync.dma_start(
                        out=ksin[:, :, :],
                        in_=io["ksinE"].rearrange("(i p) c -> p i c", p=128)), "ld", 16)
                    cx.wait(sync, "ld")
            with nc.psum_tensor("pckv", [128, 512], F32) as pckv:
                for cc in range(4):
                    for th in range(2):
                        with nc.Block() as block:
                            @block.tensor
                            def _(tensor, cc=cc, th=th):
                                for k in range(NCH):
                                    tensor.matmul(pckv[:, :], wdk_bf[:, k, ts(cc, 128)],
                                                  xn1fm[:, k, ts(th, 512)],
                                                  start=(k == 0), stop=(k == NCH - 1))
                        with nc.Block() as block:
                            @block.scalar
                            def _(scalar, cc=cc, th=th):
                                scalar.copy(ckv_bf[:, cc, ts(th, 512)], pckv[:, :])
            with (
                nc.psum_tensor("pkr", [128, 512], F32) as pkr,
                nc.psum_tensor("pkc", [128, 512], F32) as pkc,
                nc.psum_tensor("pv", [128, TOK], F32) as pv,
                nc.psum_tensor("ptK0", [128, 128], BF16) as ptK0,
                nc.psum_tensor("ptK1", [128, 128], BF16) as ptK1,
            ):
                ptK = [ptK0, ptK1]
                OKV0 = cx.v["out"]
                OPT = 48 if "k8_dbg" in io else 32
                for i in range(NT):
                    with nc.Block() as block:
                        @block.tensor
                        def _(tensor, i=i):
                            for k in range(NCH):
                                tensor.matmul(pkr[:, :], xn1fm[:, k, ts(i, 128)],
                                              wkr_bf[:, k, :],
                                              start=(k == 0), stop=(k == NCH - 1))
                            for cc in range(4):
                                tensor.matmul(pkc[:, :], ckv_bf[:, cc, ts(i, 128)],
                                              wupk_bf[:, cc, :],
                                              start=(cc == 0), stop=(cc == 3))
                            for th in range(2):
                                for cc in range(4):
                                    tensor.matmul(pv[:, ts(th, 512)],
                                                  ckv_bf[:, cc, ts(i, 128)],
                                                  wupv_bf[:, cc, ts(th, 512)],
                                                  start=(cc == 0), stop=(cc == 3))
                    k8v = k8[:, :].rearrange("p (h d) -> p h d", h=HKV)
                    kr0v = kr0f[:, :].rearrange("p (h j) -> p h j", h=HKV)
                    kswv = kswp[:, :].rearrange("p (h j) -> p h j", h=HKV)
                    t1v = t1r[:, :].rearrange("p (h j) -> p h j", h=HKV)
                    t2v = t2r[:, :].rearrange("p (h j) -> p h j", h=HKV)
                    with nc.Block() as block:
                        @block.vector
                        def _(vector, i=i, k8v=k8v, kr0v=kr0v, kswv=kswv, t1v=t1v, t2v=t2v):
                            cx.chain(vector, vector.tensor_copy(kr0f[:, :], pkr[:, :]))
                            vector.tensor_copy(kswv[:, :, 0:32], kr0v[:, :, 32:64])
                            cx.chain(vector, vector.tensor_copy(kswv[:, :, 32:64],
                                                                kr0v[:, :, 0:32]))
                            vector.tensor_tensor(t1r[:, :], kr0f[:, :], kcos[:, i, :],
                                                 op=ALU.mult)
                            cx.chain(vector, vector.tensor_tensor(t2r[:, :], kswp[:, :],
                                                                  ksin[:, i, :], op=ALU.mult))
                            vector.tensor_tensor(k8v[:, :, 0:32], t1v[:, :, 0:32],
                                                 t2v[:, :, 0:32], op=ALU.add)
                            vector.tensor_tensor(k8v[:, :, 64:96], t1v[:, :, 32:64],
                                                 t2v[:, :, 32:64],
                                                 op=ALU.add).then_inc(cx.h["vv"], 1)
                        cx.v["vv"] += 1

                        @block.scalar
                        def _(scalar, i=i, k8v=k8v):
                            if i >= 1:
                                scalar.wait_ge(cx.h["out"], OKV0 + OPT * i)
                            pkcv = pkc[:, :].rearrange("p (h j) -> p h j", h=HKV)
                            scalar.copy(k8v[:, :, 32:64], pkcv[:, :, 0:32])
                            scalar.copy(k8v[:, :, 96:128], pkcv[:, :, 32:64])
                            scalar.copy(vtm[:, :], pv[:, :]).then_inc(cx.h["aa"], 1)
                        cx.v["aa"] += 1
                    with nc.Block() as block:
                        mm0 = cx.v["mm"]
                        aaT0 = cx.v["aa"]

                        @block.tensor
                        def _(tensor, i=i, mm0=mm0, aaT0=aaT0):
                            for h in range(HKV):
                                if h >= 2:
                                    tensor.wait_ge(cx.h["aa"], aaT0 + (h - 1))
                                tensor.transpose(ptK[h % 2][:, :], k8[:, ts(h, 128)],
                                                 ident[:, :]).then_inc(cx.h["mm"], 1)
                        cx.v["mm"] += HKV

                        @block.scalar
                        def _(scalar, i=i, mm0=mm0):
                            for h in range(HKV):
                                scalar.wait_ge(cx.h["mm"], mm0 + h + 1)
                                scalar.copy(kfmt[:, h, :],
                                            ptK[h % 2][:, :]).then_inc(cx.h["aa"], 1)
                        cx.v["aa"] += HKV
                    with nc.Block() as block:
                        @block.sync
                        def _(sync, i=i):
                            if "k8_dbg" in io:
                                cx.inc(sync.dma_start(out=io["k8_dbg"][i, :, :],
                                                      in_=k8[:, :]), "out", 16)
                            cx.inc(sync.dma_start(
                                out=bass.AP(io["kown"].tensor, 128 * i,
                                            [[1024, 128], [128 * 1024, HKV], [1, 128]]),
                                in_=kfmt[:, :, :]), "out", 16)
                            cx.inc(sync.dma_start(
                                out=bass.AP(io["vout"].tensor, 128 * 128 * i,
                                            [[128, 128], [1024 * 128, HKV], [1, 128]]),
                                in_=vtm[:, :].rearrange("p (h d) -> p h d", h=HKV)), "out", 16)

        with ExitStack() as s4:
            e4 = s4.enter_context
            cosq = e4(nc.sbuf_tensor("cosq_sb", [128, TOK], F32))
            sinq = e4(nc.sbuf_tensor("sinq_sb", [128, TOK], F32))
            xq_t = [e4(nc.sbuf_tensor(f"xq_t{b}", [128, D], BF16)) for b in range(2)]
            wq_buf = [e4(nc.sbuf_tensor(f"wq_buf{b}", [128, NCH, 128], BF16)) for b in range(2)]
            qsw = e4(nc.sbuf_tensor("qsw", [128, 512], F32))
            t1q = e4(nc.sbuf_tensor("t1q", [128, 512], F32))
            t2q = e4(nc.sbuf_tensor("t2q", [128, 512], F32))
            qrot = e4(nc.sbuf_tensor("qrot", [128, TOK], BF16))
            with nc.Block() as block:
                @block.sync
                def _(sync):
                    cx.inc(sync.dma_start(out=cosq[:, :], in_=io["cosq"]), "ld", 16)
                    cx.inc(sync.dma_start(out=sinq[:, :], in_=io["sinq"]), "ld", 16)
                    ii = sync.dma_start(out=xq_t[0][:, :], in_=io["xnasg"][ts(0, 128), :])
                    cx.inc(ii, "ld", 16)
            ld40 = cx.v["ld"]
            with (
                nc.psum_tensor("ptq0", [128, 128], BF16) as ptq0,
                nc.psum_tensor("ptq1", [128, 128], BF16) as ptq1,
                nc.psum_tensor("ptq2", [128, 128], BF16) as ptq2,
            ):
                ptq = [ptq0, ptq1, ptq2]
                for i in range(NT):
                    b = i % 2
                    with nc.Block() as block:
                        @block.sync
                        def _(sync, i=i, b=b):
                            sync.wait_ge(cx.h["ld"], ld40)
                            if i + 1 < NT:
                                ii = sync.dma_start(out=xq_t[1 - b][:, :],
                                                    in_=io["xnasg"][ts(i + 1, 128), :])
                                cx.inc(ii, "ld", 16)
                    with nc.Block() as block:
                        mm0 = cx.v["mm"]
                        aa0 = cx.v["aa"]

                        @block.tensor
                        def _(tensor, i=i, b=b, mm0=mm0, aa0=aa0):
                            for k in range(NCH):
                                if k >= 3:
                                    tensor.wait_ge(cx.h["aa"], aa0 + (k - 2))
                                tensor.transpose(ptq[k % 3][:, :], xq_t[b][:, ts(k, 128)],
                                                 ident[:, :]).then_inc(cx.h["mm"], 1)
                        cx.v["mm"] += NCH

                        @block.scalar
                        def _(scalar, i=i, mm0=mm0):
                            for k in range(NCH):
                                scalar.wait_ge(cx.h["mm"], mm0 + (k + 1))
                                scalar.copy(xn1fm[:, k, ts(i, 128)],
                                            ptq[k % 3][:, :]).then_inc(cx.h["aa"], 1)
                        cx.v["aa"] += NCH
                    ld40 = cx.v["ld"]
            with nc.Block() as block:
                @block.gpsimd
                def _(gpsimd):
                    ii = gpsimd.dma_start(
                        out=wq_buf[0][:, :, :],
                        in_=bass.AP(io["wqT1"].tensor, 0, [[D, 128], [128 * D, NCH], [1, 128]]))
                    cx.inc(ii, "wld", 16)
            W0 = cx.v["wld"]
            M0 = cx.v["mm"]
            V0 = cx.v["vv"]
            OQ0 = cx.v["out"]
            with (
                nc.psum_tensor("pq4", [128, TOK], F32) as pq4,
                nc.Block() as block,
            ):
                @block.gpsimd
                def _(gpsimd):
                    for qh in range(1, NQH):
                        gpsimd.wait_ge(cx.h["mm"], M0 + 2 * (qh - 1))
                        ii = gpsimd.dma_start(
                            out=wq_buf[qh % 2][:, :, :],
                            in_=bass.AP(io["wqT1"].tensor, qh * 128,
                                        [[D, 128], [128 * D, NCH], [1, 128]]))
                        cx.inc(ii, "wld", 16)

                @block.tensor
                def _(tensor):
                    for qh in range(NQH):
                        tensor.wait_ge(cx.h["wld"], W0 + 16 * qh)
                        if qh >= 1:
                            tensor.wait_ge(cx.h["vv"], V0 + 6 * qh)
                        for th in range(2):
                            for k in range(NCH):
                                ii = tensor.matmul(pq4[:, ts(th, 512)], wq_buf[qh % 2][:, k, :],
                                                   xn1fm[:, k, ts(th, 512)],
                                                   start=(k == 0), stop=(k == NCH - 1))
                                if k == NCH - 1:
                                    ii.then_inc(cx.h["mm"], 1)
                cx.v["mm"] += 2 * NQH

                @block.vector
                def _(vector):
                    for qh in range(NQH):
                        for th in range(2):
                            vector.wait_ge(cx.h["mm"], M0 + 2 * qh + th + 1)
                            if qh >= 1 and th == 0:
                                vector.wait_ge(cx.h["out"], OQ0 + 16 * qh)
                            sl = ts(th, 512)
                            cx.chain(vector, vector.tensor_copy(qsw[0:64, :],
                                                                pq4[64:128, sl]))
                            cx.chain(vector, vector.tensor_copy(qsw[64:128, :],
                                                                pq4[0:64, sl]))
                            vector.tensor_tensor(t2q[:, :], pq4[:, sl], cosq[:, sl],
                                                 op=ALU.mult)
                            cx.chain(vector, vector.tensor_tensor(t1q[:, :], qsw[:, :],
                                                                  sinq[:, sl], op=ALU.mult))
                            vector.tensor_tensor(qrot[:, sl], t1q[:, :], t2q[:, :],
                                                 op=ALU.add).then_inc(cx.h["vv"], 3)
                cx.v["vv"] += 6 * NQH

                @block.sync
                def _(sync):
                    for qh in range(NQH):
                        sync.wait_ge(cx.h["vv"], V0 + 6 * (qh + 1))
                        ii = sync.dma_start(out=io["qasg_d"][qh, :, :], in_=qrot[:, :])
                        cx.inc(ii, "out", 16)
            with nc.Block() as block:
                @block.sync
                def _(sync):
                    cx.wait(sync, "out")
    return nc



def _segs_for(T):
    c0 = 64 * (T // 4)
    if c0 < 512:
        return [(c0, 512), (512, 1024)]
    return [(c0, 1024)]


CSTAB = 25.0


def build_phaseE(nc, io, h_limit=HKV, g_limit=4):
    st = ExitStack()
    with st:
        e = st.enter_context
        cx = Ctx(nc, st)
        for nme in ["sS", "sM", "sE", "sA", "sR", "sN", "sD", "sV2"]:
            cx.h[nme] = st.enter_context(nc.semaphore(nme))
            cx.v[nme] = 0

        q_sb = e(nc.sbuf_tensor("q_sb", [128, NQH, TOK], BF16))
        mask_sb = e(nc.sbuf_tensor("mask_sb", [128, 4, 64], F32))
        kasg_sb = e(nc.sbuf_tensor("kasg_sb", [128, HKV, TOK], BF16))
        onesc = e(nc.sbuf_tensor("onesc", [128, 1], BF16))
        onesr = e(nc.sbuf_tensor("onesr", [1, 128], BF16))
        ddb = e(nc.sbuf_tensor("ddb", [128, TOK], BF16))
        negbr = e(nc.sbuf_tensor("negbr", [1, TOK], BF16))
        sums_f = e(nc.sbuf_tensor("sums_f", [1, TOK], F32))
        rs_f = e(nc.sbuf_tensor("rs_f", [1, TOK], F32))
        rs_bf = e(nc.sbuf_tensor("rs_bf", [1, TOK], BF16))
        stE = ExitStack()
        eE = stE.enter_context
        kh_sb = [eE(nc.sbuf_tensor(f"kh{b}", [128, S], BF16)) for b in range(2)]
        vh_sb = [eE(nc.sbuf_tensor(f"vh{b}", [128, 64, 128], BF16)) for b in range(2)]
        pt_sb = [eE(nc.sbuf_tensor(f"pt{b}", [128, TOK], BF16)) for b in range(2)]
        rsb_sb = eE(nc.sbuf_tensor("rsb_sb", [128, TOK], F32))

        with nc.Block() as block:
            @block.sync
            def _(sync):
                cx.inc(sync.dma_start(out=q_sb[:, :, :],
                       in_=io["qasg"].rearrange("q p t -> p q t")), "ld", 16)
                cx.inc(sync.dma_start(out=mask_sb[:, :, :],
                       in_=io["maskT"].rearrange("(r p) q -> p r q", p=128)), "ld", 16)
                cx.inc(sync.dma_start(out=kasg_sb[:, :, :],
                       in_=io["kasg"].rearrange("h p t -> p h t")), "ld", 16)
                cx.inc(sync.dma_start(out=kh_sb[0][:, :], in_=io["kfull"][0, :, :]), "ld", 16)
                cx.inc(sync.dma_start(out=vh_sb[0][:, :, :],
                       in_=io["vfull"][0, :, :].rearrange("(T p) d -> p T d", p=128)), "ld", 16)
                cx.wait(sync, "ld")
            @block.gpsimd
            def _(gpsimd):
                gpsimd.memset(onesc[:, :], 1.0)
                gpsimd.memset(onesr[:, :], 1.0)

        with (
            nc.psum_tensor("psS0a", [128, 512], F32) as s0a,
            nc.psum_tensor("psS0b", [128, 512], F32) as s0b,
            nc.psum_tensor("psS1a", [128, 512], F32) as s1a,
            nc.psum_tensor("psS1b", [128, 512], F32) as s1b,
            nc.psum_tensor("pAV", [128, TOK], F32) as pAV,
            nc.psum_tensor("pSUM", [1, TOK], F32) as pSUM,
        ):
            psS = [[s0a, s0b], [s1a, s1b]]
            for h in range(h_limit):
                hb = h % 2
                with nc.Block() as block:
                    S0 = cx.v["sS"]; Mk0 = cx.v["sM"]; E0 = cx.v["sE"]
                    Aa0 = cx.v["sA"]; R0 = cx.v["sR"]; N0 = cx.v["sN"]
                    D0 = cx.v["sD"]; Q0 = cx.v["sV2"]
                    Mb0 = cx.v["mm"]; Ab0 = cx.v["aa"]

                    @block.sync
                    def _(sync, h=h, hb=hb):
                        if h + 1 < HKV:
                            cx.inc(sync.dma_start(out=kh_sb[1 - hb][:, :],
                                                  in_=io["kfull"][h + 1, :, :]), "ld", 16)
                            cx.inc(sync.dma_start(
                                out=vh_sb[1 - hb][:, :, :],
                                in_=io["vfull"][h + 1, :, :].rearrange(
                                    "(T p) d -> p T d", p=128)), "ld", 16)

                    def sumav(tensor, g, T, h=h, hb=hb):
                        for (cs, ce) in _segs_for(T):
                            si = 0 if cs < 512 else 1
                            stop_s = (T == (31 if si == 0 else 63))
                            tensor.matmul(pSUM[:, cs:ce], onesc[:, :],
                                          pt_sb[T % 2][:, cs:ce],
                                          start=(T == 0), stop=stop_s)
                        ii = None
                        for (cs, ce) in _segs_for(T):
                            si = 0 if cs < 512 else 1
                            stop_s = (T == (31 if si == 0 else 63))
                            ii = tensor.matmul(pAV[:, cs:ce], vh_sb[hb][:, T, :],
                                               pt_sb[T % 2][:, cs:ce],
                                               start=(T == 0), stop=stop_s)
                        ii.then_inc(cx.h["sA"], 1)

                    @block.tensor
                    def _(tensor, h=h, hb=hb, S0=S0, E0=E0, N0=N0, R0=R0, Mb0=Mb0,
                          D0=D0, Q0=Q0, sumav=sumav):
                        ld_target = 80 + 32 * h
                        tensor.wait_ge(cx.h["ld"], ld_target)
                        for g in range(g_limit):
                            qh = 4 * h + g
                            if g > 0:
                                tensor.wait_ge(cx.h["sN"], N0 + g)
                            tensor.wait_ge(cx.h["sV2"], Q0 + g + 1)
                            tensor.matmul(psS[0][0][0:1, :], onesc[:, :], ddb[:, 0:512],
                                          start=True, stop=True)
                            tensor.matmul(psS[0][1][0:1, :], onesc[:, :], ddb[:, 512:1024],
                                          start=True, stop=True).then_inc(cx.h["mm"], 1)
                            tensor.wait_ge(cx.h["sD"], D0 + g + 1)
                            for T in range(64):
                                if T >= 2:
                                    tensor.wait_ge(cx.h["sE"], E0 + g * 64 + T - 1)
                                    sumav(tensor, g, T - 2)
                                ii = None
                                for (cs, ce) in _segs_for(T):
                                    si = 0 if cs < 512 else 1
                                    pS = psS[T % 2][si]
                                    tensor.matmul(pS[:, cs - 512 * si:ce - 512 * si],
                                                  kh_sb[hb][:, ts(T, 128)],
                                                  q_sb[:, qh, cs:ce],
                                                  start=True, stop=False)
                                    ii = tensor.matmul(pS[:, cs - 512 * si:ce - 512 * si],
                                                       onesr[:, :], negbr[:, cs:ce],
                                                       start=False, stop=True)
                                ii.then_inc(cx.h["sS"], 1)
                            for T in (62, 63):
                                tensor.wait_ge(cx.h["sE"], E0 + g * 64 + T + 1)
                                sumav(tensor, g, T)
                            tensor.wait_ge(cx.h["sR"], R0 + g + 1)
                            tensor.matmul(psS[0][0][:, :], onesr[:, :], rs_bf[:, 0:512],
                                          start=True, stop=True)
                            tensor.matmul(psS[0][1][:, :], onesr[:, :], rs_bf[:, 512:1024],
                                          start=True, stop=True).then_inc(cx.h["mm"], 1)
                    cx.v["sS"] += 64 * g_limit
                    cx.v["sA"] += 64 * g_limit
                    cx.v["mm"] += 2 * g_limit

                    @block.vector
                    def _(vector, h=h, S0=S0, Aa0=Aa0, Ab0=Ab0, Mb0=Mb0, D0=D0, Q0=Q0):
                        for g in range(g_limit):
                            qh = 4 * h + g
                            vector.tensor_tensor(ddb[:, :], q_sb[:, qh, :],
                                                 kasg_sb[:, h, :],
                                                 op=ALU.mult).then_inc(cx.h["sV2"], 1)
                            vector.wait_ge(cx.h["mm"], Mb0 + 2 * g + 1)
                            vector.tensor_scalar(negbr[:, 0:512], psS[0][0][0:1, :],
                                                 -1.0, -CSTAB, op0=ALU.mult, op1=ALU.add)
                            vector.tensor_scalar(negbr[:, 512:1024], psS[0][1][0:1, :],
                                                 -1.0, -CSTAB, op0=ALU.mult,
                                                 op1=ALU.add).then_inc(cx.h["sD"], 1)
                            for T in range(64):
                                vector.wait_ge(cx.h["sS"], S0 + g * 64 + T + 1)
                                c0 = 64 * (T // 4)
                                si = 0 if c0 < 512 else 1
                                off = c0 - 512 * si
                                vector.tensor_tensor(
                                    psS[T % 2][si][:, off:off + 64],
                                    psS[T % 2][si][:, off:off + 64],
                                    mask_sb[:, T % 4, :], op=ALU.add).then_inc(cx.h["sM"], 1)
                            vector.wait_ge(cx.h["sA"], Aa0 + (g + 1) * 64)
                            cx.chain(vector, vector.tensor_copy(sums_f[:, :], pSUM[:, :]))
                            cx.chain(vector, vector.reciprocal(rs_f[:, :], sums_f[:, :]))
                            vector.tensor_copy(rs_bf[:, :], rs_f[:, :]).then_inc(cx.h["sR"], 1)
                            vector.wait_ge(cx.h["aa"], Ab0 + g + 1)
                            vector.tensor_tensor(q_sb[:, qh, 0:512], pAV[:, 0:512],
                                                 rsb_sb[:, 0:512], op=ALU.mult)
                            vector.tensor_tensor(q_sb[:, qh, 512:1024], pAV[:, 512:1024],
                                                 rsb_sb[:, 512:1024],
                                                 op=ALU.mult).then_inc(cx.h["sN"], 1)
                    cx.v["sM"] += 64 * g_limit
                    cx.v["sR"] += g_limit
                    cx.v["sN"] += g_limit
                    cx.v["sV2"] += g_limit
                    cx.v["sD"] += g_limit

                    @block.scalar
                    def _(scalar, h=h, Mk0=Mk0, Aa0=Aa0, Mb0=Mb0):
                        for g in range(g_limit):
                            for T in range(64):
                                scalar.wait_ge(cx.h["sM"], Mk0 + g * 64 + T + 1)
                                gT = g * 64 + T
                                if gT >= 2:
                                    scalar.wait_ge(cx.h["sA"], Aa0 + gT - 1)
                                ii = None
                                for (cs, ce) in _segs_for(T):
                                    si = 0 if cs < 512 else 1
                                    ii = scalar.activation(
                                        pt_sb[T % 2][:, cs:ce],
                                        psS[T % 2][si][:, cs - 512 * si:ce - 512 * si],
                                        AF.Exp, bias=0.0, scale=1.0)
                                ii.then_inc(cx.h["sE"], 1)
                            scalar.wait_ge(cx.h["mm"], Mb0 + 2 * g + 2)
                            scalar.copy(rsb_sb[:, 0:512], psS[0][0][:, :])
                            scalar.copy(rsb_sb[:, 512:1024],
                                        psS[0][1][:, :]).then_inc(cx.h["aa"], 1)
                    cx.v["sE"] += 64 * g_limit
                    cx.v["aa"] += g_limit

        stE.close()
        wo1_buf = [e(nc.sbuf_tensor(f"wo1_buf{b}", [128, NCH, 512], BF16)) for b in range(2)]
        wos = [e(nc.sbuf_tensor(f"wos{b}", [128, 512], F32)) for b in range(2)]
        with nc.Block() as block:
            @block.gpsimd
            def _(gpsimd):
                cx.inc(gpsimd.dma_start(
                    out=wo1_buf[0][:, :, :],
                    in_=bass.AP(io["woT1"].tensor, 0, [[D, 128], [128 * D, NCH], [1, 512]])),
                    "wld", 16)
        M0 = cx.v["mm"]; V0 = cx.v["vv"]; O0 = cx.v["out"]; W0 = cx.v["wld"]
        with (
            nc.psum_tensor("poE0", [128, 512], F32) as po0,
            nc.psum_tensor("poE1", [128, 512], F32) as po1,
            nc.Block() as block,
        ):
            pol = [po0, po1]

            @block.gpsimd
            def _(gpsimd):
                for g in range(1, 8):
                    gpsimd.wait_ge(cx.h["mm"], M0 + 8 * (g - 1))
                    cx.inc(gpsimd.dma_start(
                        out=wo1_buf[g % 2][:, :, :],
                        in_=bass.AP(io["woT1"].tensor, g * 512,
                                    [[D, 128], [128 * D, NCH], [1, 512]])), "wld", 16)

            @block.tensor
            def _(tensor):
                for n in range(64):
                    g, i = divmod(n, 8)
                    if i == 0:
                        tensor.wait_ge(cx.h["wld"], W0 + 16 * g)
                    if n >= 2:
                        tensor.wait_ge(cx.h["vv"], V0 + n - 1)
                    pp = pol[n % 2]
                    ii = None
                    for qh in range(NQH):
                        ii = tensor.matmul(pp[:, :], q_sb[:, qh, ts(i, 128)],
                                           wo1_buf[g % 2][:, qh, :],
                                           start=(qh == 0), stop=(qh == NQH - 1))
                    ii.then_inc(cx.h["mm"], 1)
            cx.v["mm"] += 64

            @block.vector
            def _(vector):
                for n in range(64):
                    vector.wait_ge(cx.h["mm"], M0 + n + 1)
                    if n >= 2:
                        vector.wait_ge(cx.h["out"], O0 + 16 * (n - 1))
                    vector.tensor_copy(wos[n % 2][:, :],
                                       pol[n % 2][:, :]).then_inc(cx.h["vv"], 1)
            cx.v["vv"] += 64

            @block.sync
            def _(sync):
                for n in range(64):
                    g, i = divmod(n, 8)
                    sync.wait_ge(cx.h["vv"], V0 + n + 1)
                    cx.inc(sync.dma_start(out=io["wo_asg"][ts(i, 128), ts(g, 512)],
                                          in_=wos[n % 2][:, :]), "out", 16)
        with nc.Block() as block:
            @block.sync
            def _(sync):
                cx.wait(sync, "out")
    return nc



def build_phaseF(nc, io):
    st = ExitStack()
    with st:
        e = st.enter_context
        cx = Ctx(nc, st)
        xn1fm = e(nc.sbuf_tensor("xn1fmF", [128, NCH, TOK], BF16))
        hsc = e(nc.sbuf_tensor("hscF", [128, 4, TOK], BF16))
        worg_buf = [e(nc.sbuf_tensor(f"worg{b}", [128, NCH, 512], BF16)) for b in range(2)]
        ust_buf = [e(nc.sbuf_tensor(f"ust{b}", [128, 4, 512], BF16)) for b in range(2)]
        x1sl = [e(nc.sbuf_tensor(f"x1slF{b}", [128, 512], F32)) for b in range(2)]
        wosl = [e(nc.sbuf_tensor(f"woslF{b}", [128, 512], F32)) for b in range(2)]
        t1s = e(nc.sbuf_tensor("t1sF", [128, 512], F32))
        ysl = [e(nc.sbuf_tensor(f"yslF{b}", [128, 512], F32)) for b in range(2)]

        with nc.Block() as block:
            @block.sync
            def _(sync):
                cx.inc(sync.dma_start(out=xn1fm[:, :, :], in_=io["xn1fm"]), "ld", 16)
                cx.inc(sync.dma_start(out=hsc[:, :, :], in_=io["hsc_d"]), "ld", 16)
                cx.inc(sync.dma_start(out=x1sl[0][:, :], in_=io["x1"][ts(0, 128), 0:512]),
                       "ld", 16)
                cx.inc(sync.dma_start(out=wosl[0][:, :], in_=io["wo_own"][ts(0, 128), 0:512]),
                       "ld", 16)
            @block.gpsimd
            def _(gpsimd):
                cx.inc(gpsimd.dma_start(
                    out=worg_buf[0][:, :, :],
                    in_=bass.AP(io["worgT"].tensor, 0, [[D, 128], [128 * D, NCH], [1, 512]])),
                    "wld", 16)
                cx.inc(gpsimd.dma_start(
                    out=ust_buf[0][:, :, :],
                    in_=bass.AP(io["ustack"].tensor, 0, [[D, 128], [128 * D, 4], [1, 512]])),
                    "wld", 16)

        M0 = cx.v["mm"]; V0 = cx.v["vv"]; L0 = cx.v["ld"]; O0 = cx.v["out"]
        W0 = cx.v["wld"]
        with (
            nc.psum_tensor("poF0", [128, 512], F32) as po0,
            nc.psum_tensor("poF1", [128, 512], F32) as po1,
            nc.Block() as block,
        ):
            pol = [po0, po1]

            @block.gpsimd
            def _(gpsimd):
                for g in range(1, 8):
                    gpsimd.wait_ge(cx.h["mm"], M0 + 8 * (g - 1))
                    cx.inc(gpsimd.dma_start(
                        out=worg_buf[g % 2][:, :, :],
                        in_=bass.AP(io["worgT"].tensor, g * 512,
                                    [[D, 128], [128 * D, NCH], [1, 512]])), "wld", 16)
                    cx.inc(gpsimd.dma_start(
                        out=ust_buf[g % 2][:, :, :],
                        in_=bass.AP(io["ustack"].tensor, g * 512,
                                    [[D, 128], [128 * D, 4], [1, 512]])), "wld", 16)

            @block.tensor
            def _(tensor):
                tensor.wait_ge(cx.h["ld"], L0)
                for n in range(64):
                    g, i = divmod(n, 8)
                    if i == 0:
                        tensor.wait_ge(cx.h["wld"], W0 + 32 * g)
                    if n >= 2:
                        tensor.wait_ge(cx.h["vv"], V0 + 2 * n - 3)
                    pp = pol[n % 2]
                    for k in range(NCH):
                        tensor.matmul(pp[:, :], xn1fm[:, k, ts(i, 128)],
                                      worg_buf[g % 2][:, k, :],
                                      start=(k == 0), stop=False)
                    for g2 in range(4):
                        ii = tensor.matmul(pp[:, :], hsc[:, g2, ts(i, 128)],
                                           ust_buf[g % 2][:, g2, :],
                                           start=False, stop=(g2 == 3))
                    ii.then_inc(cx.h["mm"], 1)
            cx.v["mm"] += 64

            @block.vector
            def _(vector):
                for n in range(64):
                    vector.wait_ge(cx.h["mm"], M0 + n + 1)
                    vector.wait_ge(cx.h["ld"], L0 + 32 * n)
                    vector.tensor_tensor(t1s[:, :], pol[n % 2][:, :], x1sl[n % 2][:, :],
                                         op=ALU.add).then_inc(cx.h["vv"], 1)
                    vector.wait_ge(cx.h["vv"], V0 + 2 * n + 1)
                    if n >= 2:
                        vector.wait_ge(cx.h["out"], O0 + 16 * (n - 1))
                    vector.tensor_tensor(ysl[n % 2][:, :], t1s[:, :], wosl[n % 2][:, :],
                                         op=ALU.add).then_inc(cx.h["vv"], 1)
            cx.v["vv"] += 128

            @block.sync
            def _(sync):
                for n in range(64):
                    if n + 1 < 64:
                        gg, ii2 = divmod(n + 1, 8)
                        if n >= 1:
                            sync.wait_ge(cx.h["vv"], V0 + 2 * n)
                        cx.inc(sync.dma_start(out=x1sl[(n + 1) % 2][:, :],
                                              in_=io["x1"][ts(ii2, 128), ts(gg, 512)]),
                               "ld", 16)
                        cx.inc(sync.dma_start(out=wosl[(n + 1) % 2][:, :],
                                              in_=io["wo_own"][ts(ii2, 128), ts(gg, 512)]),
                               "ld", 16)
                    g, i = divmod(n, 8)
                    sync.wait_ge(cx.h["vv"], V0 + 2 * n + 2)
                    cx.inc(sync.dma_start(out=io["y"][ts(i, 128), ts(g, 512)],
                                          in_=ysl[n % 2][:, :]), "out", 16)
        with nc.Block() as block:
            @block.sync
            def _(sync):
                cx.wait(sync, "out")
    return nc



def _f32(a):
    return np.ascontiguousarray(a, dtype=np.float32)


def host_prep_shared(W):
    p = {}
    p["idmat"] = np.eye(128, dtype=np.float32)
    wq = W["L0_attn_Wq"]
    sel = np.concatenate([np.arange(512 * a, 512 * a + 128) for a in range(8)])
    p["wqselT"] = _f32(wq[sel, :].T)
    p["bqsel_pc"] = _f32(W["L0_attn_bq"][sel].reshape(8, 128).T)
    p["m_pc"] = _f32(W["L0_attn_M"].transpose(1, 0, 2).reshape(128, 8 * 128))
    p["z_pc"] = _f32(W["L0_attn_z"].T)
    p["rwT"] = _f32(W["L0_ffn_router_weight"].T)
    p["rbias"] = _f32(W["L0_ffn_router_bias"].reshape(1, 64))
    p["eb"] = _f32(W["L0_ffn_expert_bias"])
    p["woT0"] = _f32(W["L0_attn_Wo"].T)
    p["wkrT"] = _f32(W["L1_attn_Wkr"].T)
    p["wdkT"] = _f32(W["L1_attn_Wdk"].T)
    p["wupkT"] = _f32(W["L1_attn_Wupk"].T)
    p["wupvT"] = _f32(W["L1_attn_Wupv"].T)
    p["wqT1"] = _f32(W["L1_attn_Wq"].T)
    p["woT1"] = _f32(W["L1_attn_Wo"].T)
    p["worgT"] = _f32(W["L1_ffn_W_org"].T)
    p["wql1T"] = _f32(W["L1_ffn_Wq"].T)
    p["lkT"] = _f32(W["L1_ffn_lora_keys"].T)
    p["ld2"] = _f32(W["L1_ffn_lora_down"].transpose(1, 0, 2).reshape(4096, 512))
    p["ustack"] = _f32(W["L1_ffn_lora_up"].reshape(512, 4096))
    return p


def host_prep_core(c):
    q = {}
    invf = 1.0 / (10000.0 ** (np.arange(0, 128, 2, dtype=np.float64) / 128.0))
    pos_asg = np.concatenate([np.arange(64) + (8 * j + c) * 64 for j in range(NG)])
    A = invf[:, None] * pos_asg[None, :]
    s = 1.0 / np.sqrt(128.0)
    q["cosq"] = _f32(np.vstack([np.cos(A), np.cos(A)]) * s)
    q["sinq"] = _f32(np.vstack([-np.sin(A), np.sin(A)]) * s)
    pos_own = TOK * c + np.arange(TOK)
    B = pos_own[:, None] * invf[None, :32]
    q["kcosE"] = _f32(np.tile(np.concatenate([np.cos(B), np.cos(B)], 1), (1, HKV)))
    q["ksinE"] = _f32(np.tile(np.concatenate([-np.sin(B), np.sin(B)], 1), (1, HKV)))
    kk = np.arange(512)[:, None]
    qq = np.arange(64)[None, :]
    q["maskT"] = _f32(np.where(kk <= 64 * c + qq, 0.0, -1e9))
    q["pos_asg"] = pos_asg
    return q


F32_ = np.float32


def _declare(nc, specs):
    io = {}
    for name, shape, dt, kind in specs:
        t = nc.dram_tensor(name, shape, dt, kind=kind) if kind else nc.dram_tensor(name, shape, dt)
        io[name] = t.ap()
    return io


def kernel(**inputs):
    W = {k: np.asarray(v) for k, v in inputs.items()}
    x_full = np.ascontiguousarray(W["L0_attn_hidden_states"][0])
    p = host_prep_shared(W)
    pcs = [host_prep_core(c) for c in range(NCORE)]
    EI, EO = "ExternalInput", "ExternalOutput"

    ncA = bass.Bass("TRN2", target_bir_lowering=False, detect_race_conditions=False)
    ioA = _declare(ncA, [
        ("x", [TOK, D], F32, EI), ("idmat", [128, 128], F32, EI),
        ("wqselT", [4096, 1024], F32, EI), ("bqsel_pc", [128, 8], F32, EI),
        ("m_pc", [128, 1024], F32, EI), ("z_pc", [128, 8], F32, EI),
        ("rwT", [4096, 64], F32, EI), ("rbias", [1, 64], F32, EI),
        ("eb", [64, 4096], F32, EI), ("woT0", [4096, 4096], F32, EI),
        ("x1", [TOK, D], F32, EO), ("xn1tm", [TOK, D], BF16, EO),
        ("xn1fm", [128, NCH, TOK], BF16, EO), ("partials", [128, NCH], F32, EO),
        ("denb_d", [1, TOK], F32, None),
    ])
    build_phaseA(ncA, ioA)
    shared_A = {k: p[k] for k in ["idmat", "wqselT", "bqsel_pc", "m_pc", "z_pc",
                                  "rwT", "rbias", "eb", "woT0"]}
    in_mapsA = []
    for c in range(NCORE):
        m = dict(shared_A)
        m["x"] = np.ascontiguousarray(x_full[TOK * c:TOK * (c + 1)])
        in_mapsA.append(m)
    resA = run_bass_kernel_spmd(ncA, in_mapsA, core_ids=list(range(NCORE)))
    x1 = [np.asarray(resA.results[c]["x1"]) for c in range(NCORE)]
    xn1fm = [np.asarray(resA.results[c]["xn1fm"]) for c in range(NCORE)]
    partials_all = np.ascontiguousarray(
        np.stack([np.asarray(resA.results[c]["partials"]) for c in range(NCORE)]))
    xn1_full = np.concatenate([np.asarray(resA.results[c]["xn1tm"]) for c in range(NCORE)],
                              axis=0)

    ncB = bass.Bass("TRN2", target_bir_lowering=False, detect_race_conditions=False)
    ioB = _declare(ncB, [
        ("xn1fm", [128, NCH, TOK], BF16, EI), ("partials_all", [8, 128, NCH], F32, EI),
        ("xnasg", [TOK, D], BF16, EI), ("idmat", [128, 128], F32, EI),
        ("wkrT", [4096, 512], F32, EI), ("wdkT", [4096, 512], F32, EI),
        ("wupkT", [512, 512], F32, EI), ("wupvT", [512, 1024], F32, EI),
        ("wqT1", [4096, 4096], F32, EI), ("wql1T", [4096, 128], F32, EI),
        ("lkT", [128, 8], F32, EI), ("ld2", [4096, 512], F32, EI),
        ("kcosE", [1024, 512], F32, EI), ("ksinE", [1024, 512], F32, EI),
        ("cosq", [128, 1024], F32, EI), ("sinq", [128, 1024], F32, EI),
        ("kown", [8, 128, 1024], BF16, EO), ("vout", [8, 1024, 128], BF16, EO),
        ("hsc_d", [128, 4, TOK], BF16, EO), ("qasg_d", [32, 128, 1024], BF16, EO),
        ("sc8_d", [8, 1], F32, None), ("cf8_d", [1, 8], F32, None),
    ])
    build_phaseCD(ncB, ioB)
    shared_B = {k: p[k] for k in ["idmat", "wkrT", "wdkT", "wupkT", "wupvT", "wqT1",
                                  "wql1T", "lkT", "ld2"]}
    in_mapsB = []
    for c in range(NCORE):
        m = dict(shared_B)
        m["xn1fm"] = xn1fm[c]
        m["partials_all"] = partials_all
        m["xnasg"] = np.ascontiguousarray(xn1_full[pcs[c]["pos_asg"]])
        for k in ["kcosE", "ksinE", "cosq", "sinq"]:
            m[k] = pcs[c][k]
        in_mapsB.append(m)
    resB = run_bass_kernel_spmd(ncB, in_mapsB, core_ids=list(range(NCORE)))
    kfull = np.concatenate([np.asarray(resB.results[c]["kown"]) for c in range(NCORE)],
                           axis=2)
    vfull = np.concatenate([np.asarray(resB.results[c]["vout"]) for c in range(NCORE)],
                           axis=1)
    hsc = [np.asarray(resB.results[c]["hsc_d"]) for c in range(NCORE)]
    qasg = [np.asarray(resB.results[c]["qasg_d"]) for c in range(NCORE)]

    ncC = bass.Bass("TRN2", target_bir_lowering=False, detect_race_conditions=False)
    ioC = _declare(ncC, [
        ("kfull", [8, 128, S], BF16, EI), ("vfull", [8, S, 128], BF16, EI),
        ("qasg", [32, 128, 1024], BF16, EI), ("maskT", [512, 64], F32, EI),
        ("kasg", [8, 128, 1024], BF16, EI), ("woT1", [4096, 4096], F32, EI),
        ("wo_asg", [TOK, D], F32, EO),
    ])
    build_phaseE(ncC, ioC)
    in_mapsC = []
    for c in range(NCORE):
        in_mapsC.append({"kfull": kfull, "vfull": vfull, "qasg": qasg[c],
                         "maskT": pcs[c]["maskT"], "woT1": p["woT1"],
                         "kasg": np.ascontiguousarray(kfull[:, :, pcs[c]["pos_asg"]])})
    resC = run_bass_kernel_spmd(ncC, in_mapsC, core_ids=list(range(NCORE)))
    wo_asg = [np.asarray(resC.results[c]["wo_asg"]) for c in range(NCORE)]

    wo_global = np.empty((S, D), np.float32)
    for g in range(128):
        wo_global[64 * g:64 * (g + 1)] = wo_asg[g % 8][(g // 8) * 64:(g // 8) * 64 + 64]

    ncD = bass.Bass("TRN2", target_bir_lowering=False, detect_race_conditions=False)
    ioD = _declare(ncD, [
        ("x1", [TOK, D], F32, EI), ("xn1fm", [128, NCH, TOK], BF16, EI),
        ("hsc_d", [128, 4, TOK], BF16, EI), ("wo_own", [TOK, D], F32, EI),
        ("worgT", [4096, 4096], F32, EI), ("ustack", [512, 4096], F32, EI),
        ("y", [TOK, D], F32, EO),
    ])
    build_phaseF(ncD, ioD)
    in_mapsD = []
    for c in range(NCORE):
        in_mapsD.append({"x1": x1[c], "xn1fm": xn1fm[c], "hsc_d": hsc[c],
                         "wo_own": np.ascontiguousarray(wo_global[TOK * c:TOK * (c + 1)]),
                         "worgT": p["worgT"], "ustack": p["ustack"]})
    resD = run_bass_kernel_spmd(ncD, in_mapsD, core_ids=list(range(NCORE)))
    y = np.concatenate([np.asarray(resD.results[c]["y"]) for c in range(NCORE)], axis=0)
    return y[None, :, :]


# version 04 (final)
# speedup vs baseline: 1.0089x; 1.0089x over previous
import numpy as np
import ml_dtypes
from contextlib import ExitStack

import concourse.bass as bass
import concourse.mybir as mybir
from concourse.bass_utils import run_bass_kernel_spmd

F32 = mybir.dt.float32
BF16 = mybir.dt.bfloat16
AF = mybir.ActivationFunctionType
ALU = mybir.AluOpType
AX = mybir.AxisListType

NCORE = 8
S = 8192
D = 4096
TOK = 1024
NT = TOK // 128
NCH = D // 128
QSEL = 1024
E0 = 64
HKV = 8
NQH = 32
CD = 512
NG = 16

npbf = ml_dtypes.bfloat16


def ts(i, n):
    return slice(i * n, (i + 1) * n)

_uid = [0]


def _u(name):
    _uid[0] += 1
    return f"{name}_u{_uid[0]}"



class Ctx:

    def __init__(self, nc, stack):
        self.nc = nc
        self.h = {}
        self.v = {}
        for n in ["ld", "wld", "mm", "vv", "aa", "gg", "out", "cc", "ch", "ach"]:
            self.h[n] = stack.enter_context(nc.semaphore(n))
            self.v[n] = 0

    def emit_clear(self, names=None):
        nc = self.nc
        hs = [self.h[n] for n in (names or self.h)]
        with nc.Block() as block:
            @block.sync
            def _(sync):
                for hh in hs:
                    sync.sem_clear(hh)

    def inc(self, instr, name, amt=16):
        instr.then_inc(self.h[name], amt)
        self.v[name] += amt

    def wait(self, eng, name, val=None):
        eng.wait_ge(self.h[name], self.v[name] if val is None else val)

    def chain(self, eng, instr, name="ch"):
        instr.then_inc(self.h[name], 1)
        self.v[name] += 1
        eng.wait_ge(self.h[name], self.v[name])



def build_phaseA(nc, io):
    st = ExitStack()
    with st:
        e = st.enter_context
        cx = Ctx(nc, st)
        cx.emit_clear()

        ident = e(nc.sbuf_tensor(_u("ident"), [128, 128], BF16))
        epsl = e(nc.sbuf_tensor(_u("epsl"), [128, 1], F32))
        identf = e(nc.sbuf_tensor(_u("identf"), [128, 128], F32))
        rw_bf = e(nc.sbuf_tensor(_u("rw_bf"), [128, NCH, E0], BF16))
        rbias_bc = e(nc.sbuf_tensor(_u("rbias_bc"), [128, E0], F32))
        eb_bf = e(nc.sbuf_tensor(_u("eb_bf"), [64, D], BF16))
        m_bf = e(nc.sbuf_tensor(_u("m_bf"), [128, HKV, 128], BF16))
        z_bf = e(nc.sbuf_tensor(_u("z_bf"), [128, HKV], BF16))
        bqs = e(nc.sbuf_tensor(_u("bqs"), [128, HKV], F32))
        xn_fm = e(nc.sbuf_tensor(_u("xn_fm"), [128, NCH, TOK], BF16))
        wsc = e(nc.sbuf_tensor(_u("wsc"), [128, NT], F32))
        wbi = e(nc.sbuf_tensor(_u("wbi"), [128, NT], F32))
        r0t = e(nc.sbuf_tensor(_u("r0t"), [128, NT], F32))
        nmrt = e(nc.sbuf_tensor(_u("nmrt"), [128, NT], F32))
        wdT = e(nc.sbuf_tensor(_u("wdT"), [64, NT, 128], BF16))
        retr = e(nc.sbuf_tensor(_u("retr"), [128, HKV, TOK], BF16))
        partials = e(nc.sbuf_tensor(_u("partials_sb"), [128, NCH], F32))

        with nc.Block() as block:
            @block.sync
            def _(sync):
                cx.inc(sync.dma_start(out=identf[:, :], in_=io["idmat"]), "ld", 16)
                cx.inc(sync.dma_start(out=rbias_bc[:, :],
                       in_=bass.AP(io["rbias"].tensor, 0, [[0, 128], [1, E0]])), "ld", 16)
                cx.inc(sync.dma_start(out=bqs[:, :], in_=io["bqsel_pc"]), "ld", 16)
                cx.wait(sync, "ld")

            @block.gpsimd
            def _(gpsimd):
                gpsimd.memset(epsl[:, :], 1e-5)
                cx.inc(gpsimd.dma_start(out=rw_bf[:, :, :],
                       in_=io["rwT"].rearrange("(k p) e -> p k e", p=128)), "wld", 16)
                cx.inc(gpsimd.dma_start(out=eb_bf[:, :], in_=io["eb"]), "wld", 16)
                cx.inc(gpsimd.dma_start(out=m_bf[:, :, :],
                       in_=io["m_pc"].rearrange("p (h e) -> p h e", h=HKV)), "wld", 16)
                cx.inc(gpsimd.dma_start(out=z_bf[:, :], in_=io["z_pc"]), "wld", 16)
                cx.wait(gpsimd, "wld")

        with nc.Block() as block:
            @block.vector
            def _(vector):
                vector.tensor_copy(ident[:, :], identf[:, :])

        xnb = e(nc.sbuf_tensor(_u("xnb"), [128, D], BF16))
        bn6 = e(nc.sbuf_tensor(_u("bn6"), [128, 48], F32))
        bn2 = e(nc.sbuf_tensor(_u("bn2"), [128, 2], F32))
        sq = e(nc.sbuf_tensor(_u("sq"), [128, 1], F32))
        lg = e(nc.sbuf_tensor(_u("lg"), [128, E0], F32))
        nm8 = e(nc.sbuf_tensor(_u("nm8"), [128, 1], F32))
        ee = e(nc.sbuf_tensor(_u("ee"), [128, E0], F32))
        sall = e(nc.sbuf_tensor(_u("sall"), [128, 1], F32))
        t8v = e(nc.sbuf_tensor(_u("t8v"), [128, 8], F32))
        wd = e(nc.sbuf_tensor(_u("wd"), [128, E0], F32))
        s8 = e(nc.sbuf_tensor(_u("s8"), [128, 1], F32))
        rsall = e(nc.sbuf_tensor(_u("rsall"), [128, 1], F32))
        tmp1 = e(nc.sbuf_tensor(_u("tmp1"), [128, 1], F32))
        wdnb = e(nc.sbuf_tensor(_u("wdnb"), [128, E0], BF16))
        stxt = ExitStack()
        xt = [stxt.enter_context(nc.sbuf_tensor(_u(f"xt{b}"), [128, D], F32)) for b in range(2)]

        with nc.Block() as block:
            @block.sync
            def _(sync):
                ii = sync.dma_start(out=xt[0][:, :], in_=io["x"][ts(0, 128), :])
                cx.inc(ii, "ld", 16)

        with (
            nc.psum_tensor(_u("ptA0"), [128, 128], BF16) as pt0,
            nc.psum_tensor(_u("ptA1"), [128, 128], BF16) as pt1,
            nc.psum_tensor(_u("ptA2"), [128, 128], BF16) as pt2,
            nc.psum_tensor(_u("prA"), [128, E0], F32) as psr,
            nc.psum_tensor(_u("pwdT"), [64, 128], BF16) as pwdT,
        ):
            ptl = [pt0, pt1, pt2]
            for i in range(NT):
                b = i % 2
                ld_tile = 48 + 16 * (i + 1)
                with nc.Block() as block:
                    @block.sync
                    def _(sync, i=i, b=b, ld_tile=ld_tile):
                        sync.wait_ge(cx.h["ld"], ld_tile)
                        if i + 1 < NT:
                            ii = sync.dma_start(out=xt[1 - b][:, :],
                                                in_=io["x"][ts(i + 1, 128), :])
                            cx.inc(ii, "ld", 16)

                    @block.vector
                    def _(vector, i=i, b=b, ld_tile=ld_tile):
                        vector.wait_ge(cx.h["ld"], ld_tile)
                        for p in range(8):
                            vector.bn_stats(bn6[:, p * 6:(p + 1) * 6], xt[b][:, ts(p, 512)])
                with nc.Block() as block:
                    @block.vector
                    def _(vector):
                        vector.bn_aggr(bn2[:, :], bn6[:, :])
                with nc.Block() as block:
                    @block.scalar
                    def _(scalar):
                        scalar.activation(sq[:, :], bn2[:, 1:2], AF.Sqrt, bias=epsl[:, :], scale=1.0)
                with nc.Block() as block:
                    @block.vector
                    def _(vector, i=i):
                        cx.chain(vector, vector.reciprocal(r0t[:, i:i + 1], sq[:, :]))
                        cx.chain(vector, vector.tensor_tensor(tmp1[:, :], bn2[:, 0:1],
                                                              r0t[:, i:i + 1], op=ALU.mult))
                        vector.tensor_scalar(nmrt[:, i:i + 1], tmp1[:, :], -1.0, None, op0=ALU.mult)
                with nc.Block() as block:
                    @block.scalar
                    def _(scalar, i=i, b=b):
                        scalar.activation(xnb[:, :], xt[b][:, :], AF.Identity,
                                          bias=nmrt[:, i:i + 1], scale=r0t[:, i:i + 1])
                with nc.Block() as block:
                    mm0 = cx.v["mm"]
                    aa0 = cx.v["aa"]

                    @block.tensor
                    def _(tensor, i=i, mm0=mm0, aa0=aa0):
                        if i == 0:
                            cx.wait(tensor, "wld")
                        for k in range(NCH):
                            if k >= 3:
                                tensor.wait_ge(cx.h["aa"], aa0 + (k - 2))
                            tensor.transpose(ptl[k % 3][:, :], xnb[:, ts(k, 128)],
                                             ident[:, :]).then_inc(cx.h["mm"], 1)
                        for k in range(NCH):
                            tensor.wait_ge(cx.h["aa"], aa0 + (k + 1))
                            ii = tensor.matmul(psr[:, :], xn_fm[:, k, ts(i, 128)],
                                               rw_bf[:, k, :],
                                               start=(k == 0), stop=(k == NCH - 1))
                            if k == NCH - 1:
                                ii.then_inc(cx.h["mm"], 1)
                    cx.v["mm"] += NCH + 1

                    @block.scalar
                    def _(scalar, i=i, mm0=mm0):
                        for k in range(NCH):
                            scalar.wait_ge(cx.h["mm"], mm0 + (k + 1))
                            scalar.copy(xn_fm[:, k, ts(i, 128)],
                                        ptl[k % 3][:, :]).then_inc(cx.h["aa"], 1)
                    cx.v["aa"] += NCH

                    @block.vector
                    def _(vector, mm0=mm0):
                        vector.wait_ge(cx.h["mm"], mm0 + NCH + 1)
                        cx.chain(vector, vector.tensor_tensor(lg[:, :], psr[:, :], rbias_bc[:, :],
                                                              op=ALU.add))
                        cx.chain(vector, vector.reduce_max(tmp1[:, :], lg[:, :], axis=AX.X))
                        vector.tensor_scalar(nm8[:, :], tmp1[:, :], -1.0, None, op0=ALU.mult)
                with nc.Block() as block:
                    @block.scalar
                    def _(scalar):
                        scalar.activation(ee[:, :], lg[:, :], AF.Exp, bias=nm8[:, :],
                                          scale=1.0, accum_out=sall[:, :])
                with nc.Block() as block:
                    @block.vector
                    def _(vector):
                        vector.max(t8v[:, :], ee[:, :])
                with nc.Block() as block:
                    @block.vector
                    def _(vector):
                        vector.scalar_tensor_tensor(wd[:, :], ee[:, :], t8v[:, 7:8], ee[:, :],
                                                    op0=ALU.is_ge, op1=ALU.mult,
                                                    accum_out=s8[:, :])
                with nc.Block() as block:
                    @block.vector
                    def _(vector, i=i):
                        cx.chain(vector, vector.reciprocal(rsall[:, :], sall[:, :]))
                        cx.chain(vector, vector.tensor_tensor(tmp1[:, :], s8[:, :], rsall[:, :],
                                                              op=ALU.mult))
                        vector.tensor_tensor(wsc[:, i:i + 1], tmp1[:, :], r0t[:, i:i + 1],
                                             op=ALU.mult)
                        vector.tensor_tensor(wbi[:, i:i + 1], tmp1[:, :], nmrt[:, i:i + 1],
                                             op=ALU.mult)
                        vector.tensor_scalar(wdnb[:, :], wd[:, :], rsall[:, :], None, op0=ALU.mult)
                with nc.Block() as block:
                    @block.tensor
                    def _(tensor):
                        tensor.transpose(pwdT[:, :], wdnb[:, :], ident[:, :])
                with nc.Block() as block:
                    @block.scalar
                    def _(scalar, i=i):
                        scalar.copy(wdT[:, i, :], pwdT[:, :])

        stxt.close()

        st2 = ExitStack()
        wqsel_bf = st2.enter_context(nc.sbuf_tensor(_u("wqsel_bf"), [128, NCH, QSEL], BF16))
        sig = st2.enter_context(nc.sbuf_tensor(_u("sig"), [128, HKV, TOK], BF16))
        e1 = st2.enter_context(nc.sbuf_tensor(_u("e1"), [128, 512], F32))
        e2 = st2.enter_context(nc.sbuf_tensor(_u("e2"), [128, 512], F32))
        m1 = st2.enter_context(nc.sbuf_tensor(_u("m1"), [128, 512], F32))
        with nc.Block() as block:
            @block.gpsimd
            def _(gpsimd):
                ii = gpsimd.dma_start(out=wqsel_bf[:, :, :],
                                      in_=io["wqselT"].rearrange("(k p) q -> p k q", p=128))
                cx.inc(ii, "wld", 16)
        with ExitStack() as ps_st:
            psq = [ps_st.enter_context(nc.psum_tensor(_u(f"psq{a}"), [128, 512], F32))
                   for a in range(HKV)]
            with nc.Block() as block:
                mm0 = cx.v["mm"]
                vv0 = cx.v["vv"]
                aa0 = cx.v["aa"]

                @block.tensor
                def _(tensor, mm0=mm0, vv0=vv0):
                    cx.wait(tensor, "wld")
                    for th in range(2):
                        if th == 1:
                            tensor.wait_ge(cx.h["vv"], vv0 + 16)
                        for k in range(NCH):
                            for a in range(HKV):
                                ii = tensor.matmul(psq[a][:, :], wqsel_bf[:, k, ts(a, 128)],
                                                   xn_fm[:, k, ts(th, 512)],
                                                   start=(k == 0), stop=(k == NCH - 1))
                                if k == NCH - 1 and a == HKV - 1:
                                    ii.then_inc(cx.h["mm"], 1)
                cx.v["mm"] += 2

                @block.vector
                def _(vector, mm0=mm0, aa0=aa0, vv0=vv0):
                    for th in range(2):
                        vector.wait_ge(cx.h["mm"], mm0 + th + 1)
                        for a in range(HKV):
                            idx = th * HKV + a
                            if idx >= 1:
                                vector.wait_ge(cx.h["aa"], aa0 + idx)
                            vector.tensor_scalar(e1[:, :], psq[a][:, :], bqs[:, a:a + 1], 0.0,
                                                 op0=ALU.add, op1=ALU.min)
                            vector.tensor_scalar(m1[:, :], psq[a][:, :], bqs[:, a:a + 1], 0.0,
                                                 op0=ALU.add, op1=ALU.max).then_inc(cx.h["vv"], 1)
                            vector.wait_ge(cx.h["aa"], aa0 + idx + 1)
                            vector.tensor_tensor(sig[:, a, ts(th, 512)], e2[:, :], m1[:, :],
                                                 op=ALU.add).then_inc(cx.h["vv"], 1)
                cx.v["vv"] += 4 * HKV

                @block.scalar
                def _(scalar, vv0=vv0):
                    for th in range(2):
                        for a in range(HKV):
                            idx = th * HKV + a
                            scalar.wait_ge(cx.h["vv"], vv0 + 2 * idx + 1)
                            scalar.activation(e2[:, :], e1[:, :], AF.Exp,
                                              bias=0.0, scale=1.0).then_inc(cx.h["aa"], 1)
                cx.v["aa"] += 2 * HKV

        denf = st2.enter_context(nc.sbuf_tensor(_u("denf"), [1, TOK], F32))
        denr = st2.enter_context(nc.sbuf_tensor(_u("denr"), [1, TOK], F32))
        denb = st2.enter_context(nc.sbuf_tensor(_u("denb"), [128, TOK], F32))
        with (
            nc.psum_tensor(_u("pnum"), [128, TOK], F32) as pnum,
            nc.psum_tensor(_u("pden"), [1, TOK], F32) as pden,
        ):
            for a in range(HKV):
                with nc.Block() as block:
                    @block.tensor
                    def _(tensor, a=a):
                        for th in range(2):
                            tensor.matmul(pnum[:, ts(th, 512)], m_bf[:, a, :],
                                          sig[:, a, ts(th, 512)], start=True, stop=True)
                            tensor.matmul(pden[:, ts(th, 512)], z_bf[:, a:a + 1],
                                          sig[:, a, ts(th, 512)], start=True, stop=True)
                with nc.Block() as block:
                    @block.vector
                    def _(vector):
                        vector.tensor_scalar(denf[:, :], pden[:, :], 1e-6, None, op0=ALU.add)
                with nc.Block() as block:
                    @block.vector
                    def _(vector):
                        vector.reciprocal(denr[:, :], denf[:, :])
                with nc.Block() as block:
                    @block.sync
                    def _(sync):
                        ii = sync.dma_start(out=io["denb_d"], in_=denr[:, :])
                        cx.inc(ii, "out", 16)
                        cx.wait(sync, "out")
                        ii = sync.dma_start(out=denb[:, :],
                                            in_=bass.AP(io["denb_d"].tensor, 0,
                                                        [[0, 128], [1, TOK]]))
                        cx.inc(ii, "ld", 16)
                        cx.wait(sync, "ld")
                with nc.Block() as block:
                    @block.vector
                    def _(vector, a=a):
                        vector.tensor_tensor(retr[:, a, :], pnum[:, :], denb[:, :], op=ALU.mult)

        st2.close()

        st4 = ExitStack()
        e4 = st4.enter_context
        wo_buf = [e4(nc.sbuf_tensor(_u(f"wo_buf{b}"), [128, NCH, 512], BF16)) for b in range(2)]
        xsl = [e4(nc.sbuf_tensor(_u(f"xsl{b}"), [128, 512], F32)) for b in range(2)]
        scs = [e4(nc.sbuf_tensor(_u(f"scs{b}"), [128, 512], F32)) for b in range(2)]
        t1s = e4(nc.sbuf_tensor(_u("t1s"), [128, 512], F32))
        x1sl = [e4(nc.sbuf_tensor(_u(f"x1sl{b}"), [128, 512], F32)) for b in range(2)]

        M0 = cx.v["mm"]; V0 = cx.v["vv"]; A0 = cx.v["aa"]
        L0 = cx.v["ld"]; O0 = cx.v["out"]; W0 = cx.v["wld"]
        with nc.Block() as block:
            @block.gpsimd
            def _(gpsimd):
                ii = gpsimd.dma_start(
                    out=wo_buf[0][:, :, :],
                    in_=bass.AP(io["woT0"].tensor, 0, [[D, 128], [128 * D, NCH], [1, 512]]))
                cx.inc(ii, "wld", 16)

            @block.sync
            def _(sync):
                ii = sync.dma_start(out=xsl[0][:, :], in_=io["x"][ts(0, 128), 0:512])
                cx.inc(ii, "ld", 16)

        with (
            nc.psum_tensor(_u("po0"), [128, 512], F32) as po0,
            nc.psum_tensor(_u("po1"), [128, 512], F32) as po1,
            nc.Block() as block,
        ):
            pol = [po0, po1]

            @block.gpsimd
            def _(gpsimd):
                for g in range(1, 8):
                    gpsimd.wait_ge(cx.h["mm"], M0 + 8 * (g - 1))
                    ii = gpsimd.dma_start(
                        out=wo_buf[g % 2][:, :, :],
                        in_=bass.AP(io["woT0"].tensor, g * 512,
                                    [[D, 128], [128 * D, NCH], [1, 512]]))
                    cx.inc(ii, "wld", 16)

            @block.tensor
            def _(tensor):
                for n in range(64):
                    g, i = divmod(n, 8)
                    if i == 0:
                        tensor.wait_ge(cx.h["wld"], W0 + 16 * (g + 1))
                    if n >= 2:
                        tensor.wait_ge(cx.h["vv"], V0 + 2 * n - 3)
                    pp = pol[n % 2]
                    for qh in range(NQH):
                        tensor.matmul(pp[:, :], retr[:, qh // 4, ts(i, 128)],
                                      wo_buf[g % 2][:, qh, :],
                                      start=(qh == 0), stop=False)
                    tensor.matmul(pp[:, :], wdT[:, i, :], eb_bf[:, ts(g, 512)],
                                  start=False, stop=True).then_inc(cx.h["mm"], 1)
            cx.v["mm"] += 64

            @block.scalar
            def _(scalar):
                for n in range(64):
                    g, i = divmod(n, 8)
                    scalar.wait_ge(cx.h["ld"], L0 + 16 * (n + 1))
                    if n >= 2:
                        scalar.wait_ge(cx.h["vv"], V0 + 2 * n - 2)
                    scalar.activation(scs[n % 2][:, :], xsl[n % 2][:, :], AF.Identity,
                                      bias=wbi[:, i:i + 1],
                                      scale=wsc[:, i:i + 1]).then_inc(cx.h["aa"], 1)
            cx.v["aa"] += 64

            @block.vector
            def _(vector):
                for n in range(64):
                    vector.wait_ge(cx.h["mm"], M0 + n + 1)
                    vector.wait_ge(cx.h["ld"], L0 + 16 * (n + 1))
                    vector.tensor_tensor(t1s[:, :], pol[n % 2][:, :], xsl[n % 2][:, :],
                                         op=ALU.add).then_inc(cx.h["vv"], 1)
                    vector.wait_ge(cx.h["vv"], V0 + 2 * n + 1)
                    vector.wait_ge(cx.h["aa"], A0 + n + 1)
                    if n >= 2:
                        vector.wait_ge(cx.h["out"], O0 + 16 * (n - 1))
                    vector.tensor_tensor(x1sl[n % 2][:, :], t1s[:, :], scs[n % 2][:, :],
                                         op=ALU.add).then_inc(cx.h["vv"], 1)
            cx.v["vv"] += 128

            @block.sync
            def _(sync):
                for n in range(64):
                    if n + 1 < 64:
                        gg, ii2 = divmod(n + 1, 8)
                        if n >= 1:
                            sync.wait_ge(cx.h["vv"], V0 + 2 * n - 1)
                            sync.wait_ge(cx.h["aa"], A0 + n)
                        jj = sync.dma_start(out=xsl[(n + 1) % 2][:, :],
                                            in_=io["x"][ts(ii2, 128), ts(gg, 512)])
                        cx.inc(jj, "ld", 16)
                    g, i = divmod(n, 8)
                    sync.wait_ge(cx.h["vv"], V0 + 2 * n + 2)
                    jj = sync.dma_start(out=io["x1"][ts(i, 128), ts(g, 512)],
                                        in_=x1sl[n % 2][:, :])
                    cx.inc(jj, "out", 16)

        st4.close()
        x1t = [e(nc.sbuf_tensor(_u(f"x1t{b}"), [128, D], F32)) for b in range(2)]
        xn1b = e(nc.sbuf_tensor(_u("xn1b"), [128, D], BF16))
        with nc.Block() as block:
            @block.sync
            def _(sync):
                cx.wait(sync, "out")
                ii = sync.dma_start(out=x1t[0][:, :], in_=io["x1"][ts(0, 128), :])
                cx.inc(ii, "ld", 16)

        ld5 = cx.v["ld"]
        with (
            nc.psum_tensor(_u("pt5a"), [128, 128], BF16) as pt0b,
            nc.psum_tensor(_u("pt5b"), [128, 128], BF16) as pt1b,
            nc.psum_tensor(_u("pt5c"), [128, 128], BF16) as pt2b,
        ):
            ptl5 = [pt0b, pt1b, pt2b]
            for i in range(NT):
                b = i % 2
                ld_tile = ld5 + 16 * i
                with nc.Block() as block:
                    @block.sync
                    def _(sync, i=i, b=b, ld_tile=ld_tile):
                        sync.wait_ge(cx.h["ld"], ld_tile)
                        if i + 1 < NT:
                            ii = sync.dma_start(out=x1t[1 - b][:, :],
                                                in_=io["x1"][ts(i + 1, 128), :])
                            cx.inc(ii, "ld", 16)

                    @block.vector
                    def _(vector, b=b, ld_tile=ld_tile):
                        vector.wait_ge(cx.h["ld"], ld_tile)
                        for p in range(8):
                            vector.bn_stats(bn6[:, p * 6:(p + 1) * 6], x1t[b][:, ts(p, 512)])
                with nc.Block() as block:
                    @block.vector
                    def _(vector):
                        vector.bn_aggr(bn2[:, :], bn6[:, :])
                with nc.Block() as block:
                    @block.scalar
                    def _(scalar):
                        scalar.activation(sq[:, :], bn2[:, 1:2], AF.Sqrt, bias=epsl[:, :], scale=1.0)
                with nc.Block() as block:
                    @block.vector
                    def _(vector, i=i):
                        cx.chain(vector, vector.reciprocal(r0t[:, i:i + 1], sq[:, :]))
                        cx.chain(vector, vector.tensor_tensor(tmp1[:, :], bn2[:, 0:1],
                                                              r0t[:, i:i + 1], op=ALU.mult))
                        vector.tensor_scalar(nmrt[:, i:i + 1], tmp1[:, :], -1.0, None, op0=ALU.mult)
                with nc.Block() as block:
                    @block.scalar
                    def _(scalar, i=i, b=b):
                        scalar.activation(xn1b[:, :], x1t[b][:, :], AF.Identity,
                                          bias=nmrt[:, i:i + 1], scale=r0t[:, i:i + 1])
                with nc.Block() as block:
                    mm0 = cx.v["mm"]
                    aa0 = cx.v["aa"]

                    @block.tensor
                    def _(tensor, i=i, mm0=mm0, aa0=aa0):
                        for k in range(NCH):
                            if k >= 3:
                                tensor.wait_ge(cx.h["aa"], aa0 + (k - 2))
                            tensor.transpose(ptl5[k % 3][:, :], xn1b[:, ts(k, 128)],
                                             ident[:, :]).then_inc(cx.h["mm"], 1)
                    cx.v["mm"] += NCH

                    @block.scalar
                    def _(scalar, i=i, mm0=mm0):
                        for k in range(NCH):
                            scalar.wait_ge(cx.h["mm"], mm0 + (k + 1))
                            scalar.copy(xn_fm[:, k, ts(i, 128)],
                                        ptl5[k % 3][:, :]).then_inc(cx.h["aa"], 1)
                    cx.v["aa"] += NCH

                    @block.sync
                    def _(sync, i=i):
                        ii = sync.dma_start(out=io["xn1tm"][ts(i, 128), :], in_=xn1b[:, :])
                        cx.inc(ii, "out", 16)

        with nc.Block() as block:
            @block.vector
            def _(vector):
                for k in range(NCH):
                    vector.reduce_sum(partials[:, k:k + 1], xn_fm[:, k, :], axis=AX.X)
        with nc.Block() as block:
            @block.sync
            def _(sync):
                ii = sync.dma_start(out=io["xn1fm"], in_=xn_fm[:, :, :])
                cx.inc(ii, "out", 16)
                ii = sync.dma_start(out=io["partials"], in_=partials[:, :])
                cx.inc(ii, "out", 16)
                cx.wait(sync, "out")
    return nc



def build_phaseCD(nc, io):
    st = ExitStack()
    with st:
        e = st.enter_context
        cx = Ctx(nc, st)
        cx.emit_clear()

        ident = e(nc.sbuf_tensor(_u("identCD"), [128, 128], BF16))
        identf = e(nc.sbuf_tensor(_u("identfCD"), [128, 128], F32))
        xn1fm = e(nc.sbuf_tensor(_u("xn1fm_sb"), [128, NCH, TOK], BF16))
        cvec = e(nc.sbuf_tensor(_u("cvec"), [128, 8], F32))
        onesc = e(nc.sbuf_tensor(_u("onesc"), [1, 128], BF16))

        with nc.Block() as block:
            @block.sync
            def _(sync):
                cx.inc(sync.dma_start(out=identf[:, :], in_=io["idmat"]), "ld", 16)
                cx.inc(sync.dma_start(out=xn1fm[:, :, :], in_=io["xn1fm"]), "ld", 16)
                cx.wait(sync, "ld")
        with nc.Block() as block:
            @block.vector
            def _(vector):
                vector.tensor_copy(ident[:, :], identf[:, :])
            @block.gpsimd
            def _(gpsimd):
                gpsimd.memset(onesc[:, :], 1.0)

        with ExitStack() as s3:
            e3 = s3.enter_context
            wdk_bf = e3(nc.sbuf_tensor(_u("wdk_bf"), [128, NCH, 512], BF16))
            wkr_bf = e3(nc.sbuf_tensor(_u("wkr_bf"), [128, NCH, 512], BF16))
            wupk_bf = e3(nc.sbuf_tensor(_u("wupk_bf"), [128, 4, 512], BF16))
            wupv_bf = e3(nc.sbuf_tensor(_u("wupv_bf"), [128, 4, TOK], BF16))
            kcos = e3(nc.sbuf_tensor(_u("kcos_sb"), [128, NT, 512], F32))
            ksin = e3(nc.sbuf_tensor(_u("ksin_sb"), [128, NT, 512], F32))
            ckv_bf = e3(nc.sbuf_tensor(_u("ckv_bf"), [128, 4, TOK], BF16))
            kr0f = e3(nc.sbuf_tensor(_u("kr0f"), [128, 512], F32))
            kswp = e3(nc.sbuf_tensor(_u("kswp"), [128, 512], F32))
            t1r = e3(nc.sbuf_tensor(_u("t1r"), [128, 512], F32))
            t2r = e3(nc.sbuf_tensor(_u("t2r"), [128, 512], F32))
            k8 = e3(nc.sbuf_tensor(_u("k8"), [128, TOK], BF16))
            vtm = e3(nc.sbuf_tensor(_u("vtm"), [128, TOK], BF16))
            kfmt = e3(nc.sbuf_tensor(_u("kfmt"), [128, HKV, 128], BF16))
            with nc.Block() as block:
                @block.gpsimd
                def _(gpsimd):
                    cx.inc(gpsimd.dma_start(
                        out=wdk_bf[:, :, :],
                        in_=io["wdkT"].rearrange("(k p) c -> p k c", p=128)), "wld", 16)
                    cx.inc(gpsimd.dma_start(
                        out=wkr_bf[:, :, :],
                        in_=io["wkrT"].rearrange("(k p) c -> p k c", p=128)), "wld", 16)
                    cx.inc(gpsimd.dma_start(
                        out=wupk_bf[:, :, :],
                        in_=io["wupkT"].rearrange("(k p) c -> p k c", p=128)), "wld", 16)
                    cx.inc(gpsimd.dma_start(
                        out=wupv_bf[:, :, :],
                        in_=io["wupvT"].rearrange("(k p) c -> p k c", p=128)), "wld", 16)
                    cx.wait(gpsimd, "wld")
                @block.sync
                def _(sync):
                    cx.inc(sync.dma_start(
                        out=kcos[:, :, :],
                        in_=io["kcosE"].rearrange("(i p) c -> p i c", p=128)), "ld", 16)
                    cx.inc(sync.dma_start(
                        out=ksin[:, :, :],
                        in_=io["ksinE"].rearrange("(i p) c -> p i c", p=128)), "ld", 16)
                    cx.wait(sync, "ld")
            with nc.psum_tensor(_u("pckv"), [128, 512], F32) as pckv:
                for cc in range(4):
                    for th in range(2):
                        with nc.Block() as block:
                            @block.tensor
                            def _(tensor, cc=cc, th=th):
                                for k in range(NCH):
                                    tensor.matmul(pckv[:, :], wdk_bf[:, k, ts(cc, 128)],
                                                  xn1fm[:, k, ts(th, 512)],
                                                  start=(k == 0), stop=(k == NCH - 1))
                        with nc.Block() as block:
                            @block.scalar
                            def _(scalar, cc=cc, th=th):
                                scalar.copy(ckv_bf[:, cc, ts(th, 512)], pckv[:, :])
            with (
                nc.psum_tensor(_u("pkr"), [128, 512], F32) as pkr,
                nc.psum_tensor(_u("pkc"), [128, 512], F32) as pkc,
                nc.psum_tensor(_u("pv"), [128, TOK], F32) as pv,
                nc.psum_tensor(_u("ptK0"), [128, 128], BF16) as ptK0,
                nc.psum_tensor(_u("ptK1"), [128, 128], BF16) as ptK1,
            ):
                ptK = [ptK0, ptK1]
                OKV0 = cx.v["out"]
                OPT = 48 if "k8_dbg" in io else 32
                for i in range(NT):
                    with nc.Block() as block:
                        @block.tensor
                        def _(tensor, i=i):
                            for k in range(NCH):
                                tensor.matmul(pkr[:, :], xn1fm[:, k, ts(i, 128)],
                                              wkr_bf[:, k, :],
                                              start=(k == 0), stop=(k == NCH - 1))
                            for cc in range(4):
                                tensor.matmul(pkc[:, :], ckv_bf[:, cc, ts(i, 128)],
                                              wupk_bf[:, cc, :],
                                              start=(cc == 0), stop=(cc == 3))
                            for th in range(2):
                                for cc in range(4):
                                    tensor.matmul(pv[:, ts(th, 512)],
                                                  ckv_bf[:, cc, ts(i, 128)],
                                                  wupv_bf[:, cc, ts(th, 512)],
                                                  start=(cc == 0), stop=(cc == 3))
                    k8v = k8[:, :].rearrange("p (h d) -> p h d", h=HKV)
                    kr0v = kr0f[:, :].rearrange("p (h j) -> p h j", h=HKV)
                    kswv = kswp[:, :].rearrange("p (h j) -> p h j", h=HKV)
                    t1v = t1r[:, :].rearrange("p (h j) -> p h j", h=HKV)
                    t2v = t2r[:, :].rearrange("p (h j) -> p h j", h=HKV)
                    with nc.Block() as block:
                        @block.vector
                        def _(vector, i=i, k8v=k8v, kr0v=kr0v, kswv=kswv, t1v=t1v, t2v=t2v):
                            cx.chain(vector, vector.tensor_copy(kr0f[:, :], pkr[:, :]))
                            vector.tensor_copy(kswv[:, :, 0:32], kr0v[:, :, 32:64])
                            cx.chain(vector, vector.tensor_copy(kswv[:, :, 32:64],
                                                                kr0v[:, :, 0:32]))
                            vector.tensor_tensor(t1r[:, :], kr0f[:, :], kcos[:, i, :],
                                                 op=ALU.mult)
                            cx.chain(vector, vector.tensor_tensor(t2r[:, :], kswp[:, :],
                                                                  ksin[:, i, :], op=ALU.mult))
                            vector.tensor_tensor(k8v[:, :, 0:32], t1v[:, :, 0:32],
                                                 t2v[:, :, 0:32], op=ALU.add)
                            vector.tensor_tensor(k8v[:, :, 64:96], t1v[:, :, 32:64],
                                                 t2v[:, :, 32:64],
                                                 op=ALU.add).then_inc(cx.h["vv"], 1)
                        cx.v["vv"] += 1

                        @block.scalar
                        def _(scalar, i=i, k8v=k8v):
                            if i >= 1:
                                scalar.wait_ge(cx.h["out"], OKV0 + OPT * i)
                            pkcv = pkc[:, :].rearrange("p (h j) -> p h j", h=HKV)
                            scalar.copy(k8v[:, :, 32:64], pkcv[:, :, 0:32])
                            scalar.copy(k8v[:, :, 96:128], pkcv[:, :, 32:64])
                            scalar.copy(vtm[:, :], pv[:, :]).then_inc(cx.h["aa"], 1)
                        cx.v["aa"] += 1
                    with nc.Block() as block:
                        mm0 = cx.v["mm"]
                        aaT0 = cx.v["aa"]

                        @block.tensor
                        def _(tensor, i=i, mm0=mm0, aaT0=aaT0):
                            for h in range(HKV):
                                if h >= 2:
                                    tensor.wait_ge(cx.h["aa"], aaT0 + (h - 1))
                                tensor.transpose(ptK[h % 2][:, :], k8[:, ts(h, 128)],
                                                 ident[:, :]).then_inc(cx.h["mm"], 1)
                        cx.v["mm"] += HKV

                        @block.scalar
                        def _(scalar, i=i, mm0=mm0):
                            for h in range(HKV):
                                scalar.wait_ge(cx.h["mm"], mm0 + h + 1)
                                scalar.copy(kfmt[:, h, :],
                                            ptK[h % 2][:, :]).then_inc(cx.h["aa"], 1)
                        cx.v["aa"] += HKV
                    with nc.Block() as block:
                        @block.sync
                        def _(sync, i=i):
                            if "k8_dbg" in io:
                                cx.inc(sync.dma_start(out=io["k8_dbg"][i, :, :],
                                                      in_=k8[:, :]), "out", 16)
                            cx.inc(sync.dma_start(
                                out=bass.AP(io["kown"].tensor, 128 * i,
                                            [[1024, 128], [128 * 1024, HKV], [1, 128]]),
                                in_=kfmt[:, :, :]), "out", 16)
                            cx.inc(sync.dma_start(
                                out=bass.AP(io["vout"].tensor, 128 * 128 * i,
                                            [[128, 128], [1024 * 128, HKV], [1, 128]]),
                                in_=vtm[:, :].rearrange("p (h d) -> p h d", h=HKV)), "out", 16)

        if "hook_after_kv" in io:
            io["hook_after_kv"]()
        with ExitStack() as s1:
            pall = s1.enter_context(nc.sbuf_tensor(_u("pall"), [128, 8, NCH], F32))
            acc = [s1.enter_context(nc.sbuf_tensor(_u(f"acc{b}"), [128, NCH], F32)) for b in range(2)]
            mean_bf = s1.enter_context(nc.sbuf_tensor(_u("mean_bf"), [128, NCH], BF16))
            wql1_bf = s1.enter_context(nc.sbuf_tensor(_u("wql1_bf"), [128, NCH, 128], BF16))
            lk_bf = s1.enter_context(nc.sbuf_tensor(_u("lk_bf"), [128, 8], BF16))
            qcol_bf = s1.enter_context(nc.sbuf_tensor(_u("qcol_bf"), [128, 1], BF16))
            sc8 = s1.enter_context(nc.sbuf_tensor(_u("sc8"), [8, 1], F32))
            scr = s1.enter_context(nc.sbuf_tensor(_u("scr"), [1, 8], F32))
            mx1 = s1.enter_context(nc.sbuf_tensor(_u("mx1"), [1, 1], F32))
            nmx = s1.enter_context(nc.sbuf_tensor(_u("nmx"), [1, 1], F32))
            ee8 = s1.enter_context(nc.sbuf_tensor(_u("ee8"), [1, 8], F32))
            se8 = s1.enter_context(nc.sbuf_tensor(_u("se8"), [1, 1], F32))
            rc1 = s1.enter_context(nc.sbuf_tensor(_u("rc1"), [1, 1], F32))
            cf8 = s1.enter_context(nc.sbuf_tensor(_u("cf8"), [1, 8], F32))
            with nc.Block() as block:
                @block.sync
                def _(sync):
                    if "ar_wait" in io:
                        sync.wait_ge(*io["ar_wait"])
                    if "partials_sum" in io:
                        cx.inc(sync.dma_start(out=acc[0][:, :], in_=io["partials_sum"]),
                               "ld", 16)
                    else:
                        cx.inc(sync.dma_start(
                            out=pall[:, :, :],
                            in_=io["partials_all"].rearrange("r p k -> p r k")), "ld", 16)
                    cx.wait(sync, "ld")
                @block.gpsimd
                def _(gpsimd):
                    cx.inc(gpsimd.dma_start(
                        out=wql1_bf[:, :, :],
                        in_=io["wql1T"].rearrange("(k p) q -> p k q", p=128)), "wld", 16)
                    cx.inc(gpsimd.dma_start(out=lk_bf[:, :], in_=io["lkT"]), "wld", 16)
                    cx.wait(gpsimd, "wld")
            if "partials_sum" not in io:
                with nc.Block() as block:
                    @block.vector
                    def _(vector):
                        cx.chain(vector, vector.tensor_tensor(acc[0][:, :], pall[:, 0, :],
                                                              pall[:, 1, :], op=ALU.add))
                        for r in range(2, 8):
                            b = r % 2
                            cx.chain(vector, vector.tensor_tensor(acc[1 - b][:, :],
                                                                  acc[b][:, :],
                                                                  pall[:, r, :], op=ALU.add))
            with nc.Block() as block:
                @block.scalar
                def _(scalar):
                    scalar.activation(mean_bf[:, :], acc[0][:, :], AF.Identity,
                                      bias=0.0, scale=1.0 / 8192.0)
            with (
                nc.psum_tensor(_u("pql"), [128, 1], F32) as pql,
                nc.psum_tensor(_u("psc"), [8, 1], F32) as psc,
            ):
                with nc.Block() as block:
                    @block.tensor
                    def _(tensor):
                        for k in range(NCH):
                            tensor.matmul(pql[:, :], wql1_bf[:, k, :], mean_bf[:, k:k + 1],
                                          start=(k == 0), stop=(k == NCH - 1))
                with nc.Block() as block:
                    @block.scalar
                    def _(scalar):
                        scalar.copy(qcol_bf[:, :], pql[:, :])
                with nc.Block() as block:
                    @block.tensor
                    def _(tensor):
                        tensor.matmul(psc[:, :], lk_bf[:, :], qcol_bf[:, :],
                                      start=True, stop=True)
                with nc.Block() as block:
                    @block.vector
                    def _(vector):
                        vector.tensor_scalar(sc8[:, :], psc[:, :], 1.0 / float(np.sqrt(128.0)),
                                             None, op0=ALU.mult)
            with nc.Block() as block:
                @block.sync
                def _(sync):
                    cx.inc(sync.dma_start(out=io["sc8_d"], in_=sc8[:, :]), "out", 16)
                    cx.wait(sync, "out")
                    cx.inc(sync.dma_start(out=scr[:, :],
                                          in_=bass.AP(io["sc8_d"].tensor, 0, [[1, 1], [1, 8]])),
                           "ld", 16)
                    cx.wait(sync, "ld")
            with nc.Block() as block:
                @block.vector
                def _(vector):
                    cx.chain(vector, vector.reduce_max(mx1[:, :], scr[:, :], axis=AX.X))
                    vector.tensor_scalar(nmx[:, :], mx1[:, :], -1.0, None, op0=ALU.mult)
            with nc.Block() as block:
                @block.scalar
                def _(scalar):
                    scalar.activation(ee8[:, :], scr[:, :], AF.Exp, bias=nmx[:, :],
                                      scale=1.0, accum_out=se8[:, :])
            with nc.Block() as block:
                @block.vector
                def _(vector):
                    cx.chain(vector, vector.reciprocal(rc1[:, :], se8[:, :]))
                    vector.tensor_scalar(cf8[:, :], ee8[:, :], rc1[:, :], None, op0=ALU.mult)
            with nc.Block() as block:
                @block.sync
                def _(sync):
                    cx.inc(sync.dma_start(out=io["cf8_d"], in_=cf8[:, :]), "out", 16)
                    cx.wait(sync, "out")
                    cx.inc(sync.dma_start(out=cvec[:, :],
                                          in_=bass.AP(io["cf8_d"].tensor, 0, [[0, 128], [1, 8]])),
                           "ld", 16)
                    cx.wait(sync, "ld")

        with ExitStack() as s2:
            ld2_bf = s2.enter_context(nc.sbuf_tensor(_u("ld2_bf"), [128, NCH, 512], BF16))
            scg = s2.enter_context(nc.sbuf_tensor(_u("scg"), [128, 1], F32))
            hsc = s2.enter_context(nc.sbuf_tensor(_u("hsc"), [128, 4, TOK], BF16))
            with nc.Block() as block:
                @block.gpsimd
                def _(gpsimd):
                    cx.inc(gpsimd.dma_start(
                        out=ld2_bf[:, :, :],
                        in_=io["ld2"].rearrange("(k p) r -> p k r", p=128)), "wld", 16)
                    cx.wait(gpsimd, "wld")
            with nc.psum_tensor(_u("ph"), [128, 512], F32) as ph:
                for g in range(4):
                    with nc.Block() as block:
                        @block.vector
                        def _(vector, g=g):
                            cx.chain(vector, vector.tensor_copy(scg[0:64, :],
                                                                cvec[0:64, 2 * g:2 * g + 1]))
                            vector.tensor_copy(scg[64:128, :], cvec[64:128, 2 * g + 1:2 * g + 2])
                    for th in range(2):
                        with nc.Block() as block:
                            @block.tensor
                            def _(tensor, g=g, th=th):
                                for k in range(NCH):
                                    tensor.matmul(ph[:, :], ld2_bf[:, k, ts(g, 128)],
                                                  xn1fm[:, k, ts(th, 512)],
                                                  start=(k == 0), stop=(k == NCH - 1))
                        with nc.Block() as block:
                            @block.scalar
                            def _(scalar, g=g, th=th):
                                scalar.activation(hsc[:, g, ts(th, 512)], ph[:, :], AF.Identity,
                                                  bias=0.0, scale=scg[:, :])
            with nc.Block() as block:
                @block.sync
                def _(sync):
                    cx.inc(sync.dma_start(out=io["hsc_d"], in_=hsc[:, :, :]), "out", 16)

        if "hook_before_q" in io:
            io["hook_before_q"]()
        with ExitStack() as s4:
            e4 = s4.enter_context
            cosq = e4(nc.sbuf_tensor(_u("cosq_sb"), [128, TOK], F32))
            sinq = e4(nc.sbuf_tensor(_u("sinq_sb"), [128, TOK], F32))
            xq_t = [e4(nc.sbuf_tensor(_u(f"xq_t{b}"), [128, D], BF16)) for b in range(2)]
            wq_buf = [e4(nc.sbuf_tensor(_u(f"wq_buf{b}"), [128, NCH, 128], BF16)) for b in range(2)]
            qsw = e4(nc.sbuf_tensor(_u("qsw"), [128, 512], F32))
            t1q = e4(nc.sbuf_tensor(_u("t1q"), [128, 512], F32))
            t2q = e4(nc.sbuf_tensor(_u("t2q"), [128, 512], F32))
            qrot = e4(nc.sbuf_tensor(_u("qrot"), [128, TOK], BF16))
            cds = e4(nc.sbuf_tensor(_u("cds"), [1, 1], mybir.dt.int32))

            def emit_xnasg_load(sync, rc, rt, i, buf):
                if "xn1g" in io:
                    for u in range(2):
                        sync.reg_mov(rt, (2 * i + u) * 2097152)
                        sync.reg_add(rt, rt, rc)
                        cx.inc(sync.dma_start(
                            out=buf[64 * u:64 * u + 64, :],
                            in_=bass.AP(io["xn1g"].tensor, rt, [[4096, 64], [1, 4096]])),
                            "ld", 16)
                else:
                    cx.inc(sync.dma_start(out=buf[:, :], in_=io["xnasg"][ts(i, 128), :]),
                           "ld", 16)

            with nc.Block() as block:
                @block.sync
                def _(sync):
                    cx.inc(sync.dma_start(out=cosq[:, :], in_=io["cosq"]), "ld", 16)
                    cx.inc(sync.dma_start(out=sinq[:, :], in_=io["sinq"]), "ld", 16)
                    if "xn1g" in io:
                        cx.inc(sync.dma_start(out=cds[:, :], in_=io["cid"]), "ld", 16)
                        cx.wait(sync, "ld")
                    with sync.register(_u("rcd4")) as rc, sync.register(_u("rtd4")) as rt:
                        if "xn1g" in io:
                            sync.reg_load(rc, cds[:1, :1])
                            sync.reg_mul(rc, rc, 262144)
                        emit_xnasg_load(sync, rc, rt, 0, xq_t[0])
            ld40 = cx.v["ld"]
            with (
                nc.psum_tensor(_u("ptq0"), [128, 128], BF16) as ptq0,
                nc.psum_tensor(_u("ptq1"), [128, 128], BF16) as ptq1,
                nc.psum_tensor(_u("ptq2"), [128, 128], BF16) as ptq2,
            ):
                ptq = [ptq0, ptq1, ptq2]
                for i in range(NT):
                    b = i % 2
                    with nc.Block() as block:
                        @block.sync
                        def _(sync, i=i, b=b, ld40=ld40):
                            sync.wait_ge(cx.h["ld"], ld40)
                            if i + 1 < NT:
                                with sync.register(_u(f"rcd4b{i}")) as rc, \
                                    sync.register(_u(f"rtd4b{i}")) as rt:
                                    if "xn1g" in io:
                                        sync.reg_load(rc, cds[:1, :1])
                                        sync.reg_mul(rc, rc, 262144)
                                    emit_xnasg_load(sync, rc, rt, i + 1, xq_t[1 - b])
                    with nc.Block() as block:
                        mm0 = cx.v["mm"]
                        aa0 = cx.v["aa"]

                        @block.tensor
                        def _(tensor, i=i, b=b, mm0=mm0, aa0=aa0):
                            for k in range(NCH):
                                if k >= 3:
                                    tensor.wait_ge(cx.h["aa"], aa0 + (k - 2))
                                tensor.transpose(ptq[k % 3][:, :], xq_t[b][:, ts(k, 128)],
                                                 ident[:, :]).then_inc(cx.h["mm"], 1)
                        cx.v["mm"] += NCH

                        @block.scalar
                        def _(scalar, i=i, mm0=mm0):
                            for k in range(NCH):
                                scalar.wait_ge(cx.h["mm"], mm0 + (k + 1))
                                scalar.copy(xn1fm[:, k, ts(i, 128)],
                                            ptq[k % 3][:, :]).then_inc(cx.h["aa"], 1)
                        cx.v["aa"] += NCH
                    ld40 = cx.v["ld"]
            with nc.Block() as block:
                @block.gpsimd
                def _(gpsimd):
                    ii = gpsimd.dma_start(
                        out=wq_buf[0][:, :, :],
                        in_=bass.AP(io["wqT1"].tensor, 0, [[D, 128], [128 * D, NCH], [1, 128]]))
                    cx.inc(ii, "wld", 16)
            W0 = cx.v["wld"]
            M0 = cx.v["mm"]
            V0 = cx.v["vv"]
            OQ0 = cx.v["out"]
            with (
                nc.psum_tensor(_u("pq4"), [128, TOK], F32) as pq4,
                nc.Block() as block,
            ):
                @block.gpsimd
                def _(gpsimd):
                    for qh in range(1, NQH):
                        gpsimd.wait_ge(cx.h["mm"], M0 + 2 * (qh - 1))
                        ii = gpsimd.dma_start(
                            out=wq_buf[qh % 2][:, :, :],
                            in_=bass.AP(io["wqT1"].tensor, qh * 128,
                                        [[D, 128], [128 * D, NCH], [1, 128]]))
                        cx.inc(ii, "wld", 16)

                @block.tensor
                def _(tensor):
                    for qh in range(NQH):
                        tensor.wait_ge(cx.h["wld"], W0 + 16 * qh)
                        if qh >= 1:
                            tensor.wait_ge(cx.h["vv"], V0 + 6 * qh)
                        for th in range(2):
                            for k in range(NCH):
                                ii = tensor.matmul(pq4[:, ts(th, 512)], wq_buf[qh % 2][:, k, :],
                                                   xn1fm[:, k, ts(th, 512)],
                                                   start=(k == 0), stop=(k == NCH - 1))
                                if k == NCH - 1:
                                    ii.then_inc(cx.h["mm"], 1)
                cx.v["mm"] += 2 * NQH

                @block.vector
                def _(vector):
                    for qh in range(NQH):
                        for th in range(2):
                            vector.wait_ge(cx.h["mm"], M0 + 2 * qh + th + 1)
                            if qh >= 1 and th == 0:
                                vector.wait_ge(cx.h["out"], OQ0 + 16 * qh)
                            sl = ts(th, 512)
                            cx.chain(vector, vector.tensor_copy(qsw[0:64, :],
                                                                pq4[64:128, sl]))
                            cx.chain(vector, vector.tensor_copy(qsw[64:128, :],
                                                                pq4[0:64, sl]))
                            vector.tensor_tensor(t2q[:, :], pq4[:, sl], cosq[:, sl],
                                                 op=ALU.mult)
                            cx.chain(vector, vector.tensor_tensor(t1q[:, :], qsw[:, :],
                                                                  sinq[:, sl], op=ALU.mult))
                            vector.tensor_tensor(qrot[:, sl], t1q[:, :], t2q[:, :],
                                                 op=ALU.add).then_inc(cx.h["vv"], 3)
                cx.v["vv"] += 6 * NQH

                @block.sync
                def _(sync):
                    for qh in range(NQH):
                        sync.wait_ge(cx.h["vv"], V0 + 6 * (qh + 1))
                        ii = sync.dma_start(out=io["qasg_d"][qh, :, :], in_=qrot[:, :])
                        cx.inc(ii, "out", 16)
            with nc.Block() as block:
                @block.sync
                def _(sync):
                    cx.wait(sync, "out")
    return nc



def _segs_for(T):
    c0 = 64 * (T // 4)
    if c0 < 512:
        return [(c0, 512), (512, 1024)]
    return [(c0, 1024)]


def build_phaseE(nc, io, h_limit=HKV, g_limit=4):
    st = ExitStack()
    with st:
        e = st.enter_context
        cx = Ctx(nc, st)
        for nme in ["sS", "sM", "sE", "sA", "sR", "sN"]:
            cx.h[nme] = st.enter_context(nc.semaphore(nme))
            cx.v[nme] = 0
        cx.emit_clear()

        q_sb = e(nc.sbuf_tensor(_u("q_sb"), [128, NQH, TOK], BF16))
        mask_sb = e(nc.sbuf_tensor(_u("mask_sb"), [128, 4, 64], F32))
        onesc = e(nc.sbuf_tensor(_u("onesc"), [128, 1], BF16))
        onesr = e(nc.sbuf_tensor(_u("onesr"), [1, 128], BF16))
        bm20 = e(nc.sbuf_tensor(_u("bm20"), [128, 1], F32))
        sums_f = e(nc.sbuf_tensor(_u("sums_f"), [1, TOK], F32))
        rs_f = e(nc.sbuf_tensor(_u("rs_f"), [1, TOK], F32))
        rs_bf = e(nc.sbuf_tensor(_u("rs_bf"), [1, TOK], BF16))
        stE = ExitStack()
        eE = stE.enter_context
        kh_sb = [eE(nc.sbuf_tensor(_u(f"kh{b}"), [128, S], BF16)) for b in range(2)]
        vh_sb = [eE(nc.sbuf_tensor(_u(f"vh{b}"), [128, 64, 128], BF16)) for b in range(2)]
        pt_sb = [eE(nc.sbuf_tensor(_u(f"pt{b}"), [128, TOK], BF16)) for b in range(2)]
        rsb_sb = eE(nc.sbuf_tensor(_u("rsb_sb"), [128, TOK], F32))

        with nc.Block() as block:
            @block.sync
            def _(sync):
                cx.inc(sync.dma_start(out=q_sb[:, :, :],
                       in_=io["qasg"].rearrange("q p t -> p q t")), "ld", 16)
                cx.inc(sync.dma_start(out=mask_sb[:, :, :],
                       in_=io["maskT"].rearrange("(r p) q -> p r q", p=128)), "ld", 16)
                if "kg" in io:
                    cx.inc(sync.dma_start(out=kh_sb[0][:, :],
                           in_=bass.AP(io["kg"].tensor, 0,
                                       [[1024, 128], [1048576, 8], [1, 1024]])), "ld", 16)
                    for r_ in range(8):
                        jj = sync.dma_start(
                            out=vh_sb[0][:, r_ * 8:(r_ + 1) * 8, :],
                            in_=bass.AP(io["vg"].tensor, r_ * 1048576,
                                        [[128, 128], [16384, 8], [1, 128]]))
                        if r_ == 7:
                            cx.inc(jj, "ld", 16)
                        else:
                            cx.inc(jj, "gg", 16)
                else:
                    cx.inc(sync.dma_start(out=kh_sb[0][:, :], in_=io["kfull"][0, :, :]),
                           "ld", 16)
                    cx.inc(sync.dma_start(out=vh_sb[0][:, :, :],
                           in_=io["vfull"][0, :, :].rearrange("(T p) d -> p T d", p=128)),
                           "ld", 16)
                cx.wait(sync, "ld")
            @block.gpsimd
            def _(gpsimd):
                gpsimd.memset(onesc[:, :], 1.0)
                gpsimd.memset(onesr[:, :], 1.0)
                gpsimd.memset(bm20[:, :], -20.0)

        with (
            nc.psum_tensor(_u("psS0"), [128, TOK], F32) as pS0,
            nc.psum_tensor(_u("psS1"), [128, TOK], F32) as pS1,
            nc.psum_tensor(_u("pAV"), [128, TOK], F32) as pAV,
            nc.psum_tensor(_u("pSUM"), [1, TOK], F32) as pSUM,
        ):
            psS = [pS0, pS1]
            LE0 = cx.v["ld"]
            for h in range(h_limit):
                hb = h % 2
                with nc.Block() as block:
                    S0 = cx.v["sS"]; Mk0 = cx.v["sM"]; E0 = cx.v["sE"]
                    Aa0 = cx.v["sA"]; R0 = cx.v["sR"]; N0 = cx.v["sN"]
                    Mb0 = cx.v["mm"]; Ab0 = cx.v["aa"]

                    @block.sync
                    def _(sync, h=h, hb=hb):
                        if h + 1 < HKV:
                            if "kg" in io:
                                cx.inc(sync.dma_start(
                                    out=kh_sb[1 - hb][:, :],
                                    in_=bass.AP(io["kg"].tensor, (h + 1) * 131072,
                                                [[1024, 128], [1048576, 8], [1, 1024]])),
                                    "ld", 16)
                                for r_ in range(8):
                                    jj = sync.dma_start(
                                        out=vh_sb[1 - hb][:, r_ * 8:(r_ + 1) * 8, :],
                                        in_=bass.AP(io["vg"].tensor,
                                                    (h + 1) * 131072 + r_ * 1048576,
                                                    [[128, 128], [16384, 8], [1, 128]]))
                                    if r_ == 7:
                                        cx.inc(jj, "ld", 16)
                                    else:
                                        cx.inc(jj, "gg", 16)
                            else:
                                cx.inc(sync.dma_start(out=kh_sb[1 - hb][:, :],
                                                      in_=io["kfull"][h + 1, :, :]), "ld", 16)
                                cx.inc(sync.dma_start(
                                    out=vh_sb[1 - hb][:, :, :],
                                    in_=io["vfull"][h + 1, :, :].rearrange(
                                        "(T p) d -> p T d", p=128)), "ld", 16)

                    def sumav(tensor, g, T, h=h, hb=hb):
                        for (cs, ce) in _segs_for(T):
                            stop_s = (T == (31 if cs < 512 else 63))
                            tensor.matmul(pSUM[:, cs:ce], onesc[:, :],
                                          pt_sb[T % 2][:, cs:ce],
                                          start=(T == 0), stop=stop_s)
                        ii = None
                        for (cs, ce) in _segs_for(T):
                            stop_s = (T == (31 if cs < 512 else 63))
                            ii = tensor.matmul(pAV[:, cs:ce], vh_sb[hb][:, T, :],
                                               pt_sb[T % 2][:, cs:ce],
                                               start=(T == 0), stop=stop_s)
                        ii.then_inc(cx.h["sA"], 1)

                    @block.tensor
                    def _(tensor, h=h, hb=hb, S0=S0, E0=E0, N0=N0, R0=R0, sumav=sumav):
                        tensor.wait_ge(cx.h["ld"], LE0 + 32 * h)
                        for g in range(g_limit):
                            qh = 4 * h + g
                            if g > 0:
                                tensor.wait_ge(cx.h["sN"], N0 + g)
                            for T in range(64):
                                if T >= 2:
                                    tensor.wait_ge(cx.h["sE"], E0 + g * 64 + T - 1)
                                    sumav(tensor, g, T - 2)
                                c0 = 64 * (T // 4)
                                ii = None
                                for (cs, ce) in _segs_for(T):
                                    ii = tensor.matmul(psS[T % 2][:, cs:ce],
                                                       kh_sb[hb][:, ts(T, 128)],
                                                       q_sb[:, qh, cs:ce],
                                                       start=True, stop=True)
                                ii.then_inc(cx.h["sS"], 1)
                            for T in (62, 63):
                                tensor.wait_ge(cx.h["sE"], E0 + g * 64 + T + 1)
                                sumav(tensor, g, T)
                            tensor.wait_ge(cx.h["sR"], R0 + g + 1)
                            tensor.matmul(psS[0][:, 0:512], onesr[:, :], rs_bf[:, 0:512],
                                          start=True, stop=True)
                            tensor.matmul(psS[0][:, 512:1024], onesr[:, :],
                                          rs_bf[:, 512:1024],
                                          start=True, stop=True).then_inc(cx.h["mm"], 1)
                    cx.v["sS"] += 64 * g_limit
                    cx.v["sA"] += 64 * g_limit
                    cx.v["mm"] += g_limit

                    @block.vector
                    def _(vector, h=h, S0=S0, Aa0=Aa0, Ab0=Ab0, N0=N0):
                        for g in range(g_limit):
                            qh = 4 * h + g
                            for T in range(64):
                                vector.wait_ge(cx.h["sS"], S0 + g * 64 + T + 1)
                                c0 = 64 * (T // 4)
                                vector.tensor_tensor(
                                    psS[T % 2][:, c0:c0 + 64],
                                    psS[T % 2][:, c0:c0 + 64],
                                    mask_sb[:, T % 4, :], op=ALU.add).then_inc(cx.h["sM"], 1)
                            vector.wait_ge(cx.h["sA"], Aa0 + (g + 1) * 64)
                            cx.chain(vector, vector.tensor_copy(sums_f[:, :], pSUM[:, :]))
                            cx.chain(vector, vector.reciprocal(rs_f[:, :], sums_f[:, :]))
                            vector.tensor_copy(rs_bf[:, :], rs_f[:, :]).then_inc(cx.h["sR"], 1)
                            vector.wait_ge(cx.h["aa"], Ab0 + g + 1)
                            vector.tensor_tensor(q_sb[:, qh, 0:512], pAV[:, 0:512],
                                                 rsb_sb[:, 0:512], op=ALU.mult)
                            vector.tensor_tensor(q_sb[:, qh, 512:1024], pAV[:, 512:1024],
                                                 rsb_sb[:, 512:1024],
                                                 op=ALU.mult).then_inc(cx.h["sN"], 1)
                    cx.v["sM"] += 64 * g_limit
                    cx.v["sR"] += g_limit
                    cx.v["sN"] += g_limit

                    @block.scalar
                    def _(scalar, h=h, Mk0=Mk0, Aa0=Aa0, Mb0=Mb0):
                        for g in range(g_limit):
                            for T in range(64):
                                scalar.wait_ge(cx.h["sM"], Mk0 + g * 64 + T + 1)
                                gT = g * 64 + T
                                if gT >= 2:
                                    scalar.wait_ge(cx.h["sA"], Aa0 + gT - 1)
                                c0 = 64 * (T // 4)
                                scalar.activation(pt_sb[T % 2][:, c0:1024],
                                                  psS[T % 2][:, c0:1024],
                                                  AF.Exp, bias=bm20[:, :],
                                                  scale=1.0).then_inc(cx.h["sE"], 1)
                            scalar.wait_ge(cx.h["mm"], Mb0 + g + 1)
                            scalar.copy(rsb_sb[:, 0:512], psS[0][:, 0:512])
                            scalar.copy(rsb_sb[:, 512:1024],
                                        psS[0][:, 512:1024]).then_inc(cx.h["aa"], 1)
                    cx.v["sE"] += 64 * g_limit
                    cx.v["aa"] += g_limit

        stE.close()
        wo1_buf = [e(nc.sbuf_tensor(_u(f"wo1_buf{b}"), [128, NCH, 512], BF16)) for b in range(2)]
        wos = [e(nc.sbuf_tensor(_u(f"wos{b}"), [128, 512], F32)) for b in range(2)]
        with nc.Block() as block:
            @block.gpsimd
            def _(gpsimd):
                cx.inc(gpsimd.dma_start(
                    out=wo1_buf[0][:, :, :],
                    in_=bass.AP(io["woT1"].tensor, 0, [[D, 128], [128 * D, NCH], [1, 512]])),
                    "wld", 16)
        M0 = cx.v["mm"]; V0 = cx.v["vv"]; O0 = cx.v["out"]; W0 = cx.v["wld"]
        with (
            nc.psum_tensor(_u("poE0"), [128, 512], F32) as po0,
            nc.psum_tensor(_u("poE1"), [128, 512], F32) as po1,
            nc.Block() as block,
        ):
            pol = [po0, po1]

            @block.gpsimd
            def _(gpsimd):
                for g in range(1, 8):
                    gpsimd.wait_ge(cx.h["mm"], M0 + 8 * (g - 1))
                    cx.inc(gpsimd.dma_start(
                        out=wo1_buf[g % 2][:, :, :],
                        in_=bass.AP(io["woT1"].tensor, g * 512,
                                    [[D, 128], [128 * D, NCH], [1, 512]])), "wld", 16)

            @block.tensor
            def _(tensor):
                for n in range(64):
                    g, i = divmod(n, 8)
                    if i == 0:
                        tensor.wait_ge(cx.h["wld"], W0 + 16 * g)
                    if n >= 2:
                        tensor.wait_ge(cx.h["vv"], V0 + n - 1)
                    pp = pol[n % 2]
                    ii = None
                    for qh in range(NQH):
                        ii = tensor.matmul(pp[:, :], q_sb[:, qh, ts(i, 128)],
                                           wo1_buf[g % 2][:, qh, :],
                                           start=(qh == 0), stop=(qh == NQH - 1))
                    ii.then_inc(cx.h["mm"], 1)
            cx.v["mm"] += 64

            @block.vector
            def _(vector):
                for n in range(64):
                    vector.wait_ge(cx.h["mm"], M0 + n + 1)
                    if n >= 2:
                        vector.wait_ge(cx.h["out"], O0 + 16 * (n - 1))
                    vector.tensor_copy(wos[n % 2][:, :],
                                       pol[n % 2][:, :]).then_inc(cx.h["vv"], 1)
            cx.v["vv"] += 64

            @block.sync
            def _(sync):
                for n in range(64):
                    g, i = divmod(n, 8)
                    sync.wait_ge(cx.h["vv"], V0 + n + 1)
                    cx.inc(sync.dma_start(out=io["wo_asg"][ts(i, 128), ts(g, 512)],
                                          in_=wos[n % 2][:, :]), "out", 16)
        with nc.Block() as block:
            @block.sync
            def _(sync):
                cx.wait(sync, "out")
    return nc



def build_phaseF(nc, io):
    st = ExitStack()
    with st:
        e = st.enter_context
        cx = Ctx(nc, st)
        cx.emit_clear()
        xn1fm = e(nc.sbuf_tensor(_u("xn1fmF"), [128, NCH, TOK], BF16))
        hsc = e(nc.sbuf_tensor(_u("hscF"), [128, 4, TOK], BF16))
        worg_buf = [e(nc.sbuf_tensor(_u(f"worg{b}"), [128, NCH, 512], BF16)) for b in range(2)]
        ust_buf = [e(nc.sbuf_tensor(_u(f"ust{b}"), [128, 4, 512], BF16)) for b in range(2)]
        x1sl = [e(nc.sbuf_tensor(_u(f"x1slF{b}"), [128, 512], F32)) for b in range(2)]
        wosl = [e(nc.sbuf_tensor(_u(f"woslF{b}"), [128, 512], F32)) for b in range(2)]
        t1s = e(nc.sbuf_tensor(_u("t1sF"), [128, 512], F32))
        ysl = [e(nc.sbuf_tensor(_u(f"yslF{b}"), [128, 512], F32)) for b in range(2)]

        cdsF = e(nc.sbuf_tensor(_u("cdsF"), [1, 1], mybir.dt.int32))
        WOG = "wog" in io
        LD_PER = 48 if WOG else 32

        def emit_wo_load(sync, rc, rt, i, gcol, buf):
            if WOG:
                for u in range(2):
                    a = (2 * i + u) % 8
                    static = a * 4194304 + (i // 4) * 262144 + gcol * 512
                    sync.reg_mov(rt, static)
                    sync.reg_add(rt, rt, rc)
                    cx.inc(sync.dma_start(
                        out=buf[64 * u:64 * u + 64, :],
                        in_=bass.AP(io["wog"].tensor, rt, [[4096, 64], [1, 512]])),
                        "ld", 16)
            else:
                cx.inc(sync.dma_start(out=buf[:, :],
                                      in_=io["wo_own"][ts(i, 128), ts(gcol, 512)]), "ld", 16)

        holdF = {}
        with nc.Block() as block:
            @block.sync
            def _(sync):
                cx.inc(sync.dma_start(out=xn1fm[:, :, :], in_=io["xn1fm"]), "ld", 16)
                cx.inc(sync.dma_start(out=hsc[:, :, :], in_=io["hsc_d"]), "ld", 16)
                holdF["wh"] = cx.v["ld"]
                cx.inc(sync.dma_start(out=x1sl[0][:, :], in_=io["x1"][ts(0, 128), 0:512]),
                       "ld", 16)
                if WOG:
                    cx.inc(sync.dma_start(out=cdsF[:, :], in_=io["cid"]), "ld", 16)
                    cx.wait(sync, "ld")
                if "hook_wo_gather" in io:
                    io["hook_wo_gather"](sync)
                with sync.register(_u("rcF")) as rc, sync.register(_u("rtF")) as rt:
                    if WOG:
                        sync.reg_load(rc, cdsF[:1, :1])
                        sync.reg_mul(rc, rc, 524288)
                    emit_wo_load(sync, rc, rt, 0, 0, wosl[0])
            @block.gpsimd
            def _(gpsimd):
                cx.inc(gpsimd.dma_start(
                    out=worg_buf[0][:, :, :],
                    in_=bass.AP(io["worgT"].tensor, 0, [[D, 128], [128 * D, NCH], [1, 512]])),
                    "wld", 16)
                cx.inc(gpsimd.dma_start(
                    out=ust_buf[0][:, :, :],
                    in_=bass.AP(io["ustack"].tensor, 0, [[D, 128], [128 * D, 4], [1, 512]])),
                    "wld", 16)

        M0 = cx.v["mm"]; V0 = cx.v["vv"]; L0 = cx.v["ld"]; O0 = cx.v["out"]
        W0 = cx.v["wld"]
        with (
            nc.psum_tensor(_u("poF0"), [128, 512], F32) as po0,
            nc.psum_tensor(_u("poF1"), [128, 512], F32) as po1,
            nc.Block() as block,
        ):
            pol = [po0, po1]

            @block.gpsimd
            def _(gpsimd):
                for g in range(1, 8):
                    gpsimd.wait_ge(cx.h["mm"], M0 + 8 * (g - 1))
                    cx.inc(gpsimd.dma_start(
                        out=worg_buf[g % 2][:, :, :],
                        in_=bass.AP(io["worgT"].tensor, g * 512,
                                    [[D, 128], [128 * D, NCH], [1, 512]])), "wld", 16)
                    cx.inc(gpsimd.dma_start(
                        out=ust_buf[g % 2][:, :, :],
                        in_=bass.AP(io["ustack"].tensor, g * 512,
                                    [[D, 128], [128 * D, 4], [1, 512]])), "wld", 16)

            @block.tensor
            def _(tensor):
                tensor.wait_ge(cx.h["ld"], holdF["wh"])
                for n in range(64):
                    g, i = divmod(n, 8)
                    if i == 0:
                        tensor.wait_ge(cx.h["wld"], W0 + 32 * g)
                    if n >= 2:
                        tensor.wait_ge(cx.h["vv"], V0 + 2 * n - 3)
                    pp = pol[n % 2]
                    for k in range(NCH):
                        tensor.matmul(pp[:, :], xn1fm[:, k, ts(i, 128)],
                                      worg_buf[g % 2][:, k, :],
                                      start=(k == 0), stop=False)
                    for g2 in range(4):
                        ii = tensor.matmul(pp[:, :], hsc[:, g2, ts(i, 128)],
                                           ust_buf[g % 2][:, g2, :],
                                           start=False, stop=(g2 == 3))
                    ii.then_inc(cx.h["mm"], 1)
            cx.v["mm"] += 64

            @block.vector
            def _(vector):
                for n in range(64):
                    vector.wait_ge(cx.h["mm"], M0 + n + 1)
                    vector.wait_ge(cx.h["ld"], L0 + LD_PER * n)
                    vector.tensor_tensor(t1s[:, :], pol[n % 2][:, :], x1sl[n % 2][:, :],
                                         op=ALU.add).then_inc(cx.h["vv"], 1)
                    vector.wait_ge(cx.h["vv"], V0 + 2 * n + 1)
                    if n >= 2:
                        vector.wait_ge(cx.h["out"], O0 + 16 * (n - 1))
                    vector.tensor_tensor(ysl[n % 2][:, :], t1s[:, :], wosl[n % 2][:, :],
                                         op=ALU.add).then_inc(cx.h["vv"], 1)
            cx.v["vv"] += 128

            @block.sync
            def _(sync):
                with sync.register(_u("rcF2")) as rc, sync.register(_u("rtF2")) as rt:
                    if WOG:
                        sync.reg_load(rc, cdsF[:1, :1])
                        sync.reg_mul(rc, rc, 524288)
                    for n in range(64):
                        if n + 1 < 64:
                            gg, ii2 = divmod(n + 1, 8)
                            if n >= 1:
                                sync.wait_ge(cx.h["vv"], V0 + 2 * n)
                            cx.inc(sync.dma_start(out=x1sl[(n + 1) % 2][:, :],
                                                  in_=io["x1"][ts(ii2, 128), ts(gg, 512)]),
                                   "ld", 16)
                            emit_wo_load(sync, rc, rt, ii2, gg, wosl[(n + 1) % 2])
                        g, i = divmod(n, 8)
                        sync.wait_ge(cx.h["vv"], V0 + 2 * n + 2)
                        cx.inc(sync.dma_start(out=io["y"][ts(i, 128), ts(g, 512)],
                                              in_=ysl[n % 2][:, :]), "out", 16)
        with nc.Block() as block:
            @block.sync
            def _(sync):
                cx.wait(sync, "out")
    return nc



def _f32(a):
    return np.ascontiguousarray(a, dtype=np.float32)


def host_prep_shared(W):
    p = {}
    p["idmat"] = np.eye(128, dtype=np.float32)
    wq = W["L0_attn_Wq"]
    sel = np.concatenate([np.arange(512 * a, 512 * a + 128) for a in range(8)])
    p["wqselT"] = _f32(wq[sel, :].T)
    p["bqsel_pc"] = _f32(W["L0_attn_bq"][sel].reshape(8, 128).T)
    p["m_pc"] = _f32(W["L0_attn_M"].transpose(1, 0, 2).reshape(128, 8 * 128))
    p["z_pc"] = _f32(W["L0_attn_z"].T)
    p["rwT"] = _f32(W["L0_ffn_router_weight"].T)
    p["rbias"] = _f32(W["L0_ffn_router_bias"].reshape(1, 64))
    p["eb"] = _f32(W["L0_ffn_expert_bias"])
    p["woT0"] = _f32(W["L0_attn_Wo"].T)
    p["wkrT"] = _f32(W["L1_attn_Wkr"].T)
    p["wdkT"] = _f32(W["L1_attn_Wdk"].T)
    p["wupkT"] = _f32(W["L1_attn_Wupk"].T)
    p["wupvT"] = _f32(W["L1_attn_Wupv"].T)
    p["wqT1"] = _f32(W["L1_attn_Wq"].T)
    p["woT1"] = _f32(W["L1_attn_Wo"].T)
    p["worgT"] = _f32(W["L1_ffn_W_org"].T)
    p["wql1T"] = _f32(W["L1_ffn_Wq"].T)
    p["lkT"] = _f32(W["L1_ffn_lora_keys"].T)
    p["ld2"] = _f32(W["L1_ffn_lora_down"].transpose(1, 0, 2).reshape(4096, 512))
    p["ustack"] = _f32(W["L1_ffn_lora_up"].reshape(512, 4096))
    return p


def host_prep_core(c):
    q = {}
    invf = 1.0 / (10000.0 ** (np.arange(0, 128, 2, dtype=np.float64) / 128.0))
    pos_asg = np.concatenate([np.arange(64) + (8 * j + c) * 64 for j in range(NG)])
    A = invf[:, None] * pos_asg[None, :]
    s = 1.0 / np.sqrt(128.0)
    q["cosq"] = _f32(np.vstack([np.cos(A), np.cos(A)]) * s)
    q["sinq"] = _f32(np.vstack([-np.sin(A), np.sin(A)]) * s)
    pos_own = TOK * c + np.arange(TOK)
    B = pos_own[:, None] * invf[None, :32]
    q["kcosE"] = _f32(np.tile(np.concatenate([np.cos(B), np.cos(B)], 1), (1, HKV)))
    q["ksinE"] = _f32(np.tile(np.concatenate([-np.sin(B), np.sin(B)], 1), (1, HKV)))
    kk = np.arange(512)[:, None]
    qq = np.arange(64)[None, :]
    q["maskT"] = _f32(np.where(kk <= 64 * c + qq, 0.0, -1e9))
    q["pos_asg"] = pos_asg
    return q



def build_fused(nc, io):
    EI = "ExternalInput"
    x1_d = nc.dram_tensor("x1_d", [TOK, D], F32)
    xn1tm_d = nc.dram_tensor("xn1tm_d", [TOK, D], BF16)
    xn1g_d = nc.dram_tensor("xn1g_d", [S, D], BF16)
    xn1fm_d = nc.dram_tensor("xn1fm_d", [128, NCH, TOK], BF16)
    partials_d = nc.dram_tensor("partials_d", [128, NCH], F32)
    psum_d = nc.dram_tensor("psum_d", [128, NCH], F32)
    kown_d = nc.dram_tensor("kown_d", [1024, 1024], BF16)
    kg_d = nc.dram_tensor("kg_d", [8192, 1024], BF16)
    vout_d = nc.dram_tensor("vout_d", [8192, 128], BF16)
    vg_d = nc.dram_tensor("vg_d", [65536, 128], BF16)
    hsc_d = nc.dram_tensor("hsc_d", [128, 4, TOK], BF16)
    qasg_d = nc.dram_tensor("qasg_d2", [32, 128, 1024], BF16)
    wo_asg_d = nc.dram_tensor("wo_asg_d", [TOK, D], F32)
    wog_d = nc.dram_tensor("wog_d", [8192, D], F32)
    denb_d = nc.dram_tensor("denb_d", [1, TOK], F32)
    sc8_d = nc.dram_tensor("sc8_d", [8, 1], F32)
    cf8_d = nc.dram_tensor("cf8_d", [1, 8], F32)
    RG = [list(range(NCORE))]

    with nc.semaphore(_u("ccg")) as ccg:
        ioA = {k: io[k] for k in ["x", "idmat", "wqselT", "bqsel_pc", "m_pc", "z_pc",
                                  "rwT", "rbias", "eb", "woT0"]}
        ioA.update({"x1": x1_d.ap(), "xn1tm": xn1tm_d.ap(), "xn1fm": xn1fm_d.ap(),
                    "partials": partials_d.ap(), "denb_d": denb_d.ap()})
        build_phaseA(nc, ioA)

        xnasg_d = nc.dram_tensor("xnasg_d", [TOK, D], BF16)
        wo_own_d = nc.dram_tensor("wo_own_d", [TOK, D], F32)
        cid_sb = nc.sbuf_tensor("cid_sb_g", [1, 1], mybir.dt.int32)
        with cid_sb as csb, nc.semaphore("ccg2") as ccg2:
            with nc.Block() as block:
                @block.gpsimd
                def _(gpsimd):
                    gpsimd.collective_compute(
                        "AllReduce", ALU.add, replica_groups=RG,
                        ins=[partials_d.ap()], outs=[psum_d.ap()]).then_inc(ccg2, 1)
                    gpsimd.collective_compute(
                        "AllGather", ALU.bypass, replica_groups=RG,
                        ins=[xn1tm_d.ap()], outs=[xn1g_d.ap()]).then_inc(ccg, 1)
                @block.sync
                def _(sync):
                    sync.dma_start(out=csb[:, :], in_=io["cid"]).then_inc(ccg, 16)

            def hook_after_kv():
                with nc.Block() as block:
                    @block.gpsimd
                    def _(gpsimd):
                        gpsimd.collective_compute(
                            "AllGather", ALU.bypass, replica_groups=RG,
                            ins=[kown_d.ap()], outs=[kg_d.ap()]).then_inc(ccg, 1)
                        gpsimd.collective_compute(
                            "AllGather", ALU.bypass, replica_groups=RG,
                            ins=[vout_d.ap()], outs=[vg_d.ap()]).then_inc(ccg, 1)

            def hook_before_q():
                with nc.Block() as block:
                    @block.sync
                    def _(sync):
                        sync.wait_ge(ccg, 16 + 1)
                        with sync.register("rgA") as rg:
                            sync.reg_load(rg, csb[:1, :1])
                            sync.reg_mul(rg, rg, 262144)
                            sync.dma_start(
                                out=bass.AP(xnasg_d, 0,
                                            [[262144, NG], [4096, 64], [1, 4096]]),
                                in_=bass.AP(xn1g_d, rg,
                                            [[2097152, NG], [4096, 64], [1, 4096]])) \
                                .then_inc(ccg, 16)
                        sync.wait_ge(ccg, 32 + 1)

            ioCD = {k: io[k] for k in ["idmat", "wkrT", "wdkT", "wupkT", "wupvT", "wqT1",
                                       "wql1T", "lkT", "ld2", "kcosE", "ksinE", "cosq",
                                       "sinq"]}
            ioCD.update({"xn1fm": xn1fm_d.ap(), "partials_sum": psum_d.ap(),
                         "xnasg": xnasg_d.ap(), "kown": kown_d.ap(), "vout": vout_d.ap(),
                         "hsc_d": hsc_d.ap(), "qasg_d": qasg_d.ap(),
                         "sc8_d": sc8_d.ap(), "cf8_d": cf8_d.ap(),
                         "ar_wait": (ccg2, 1),
                         "hook_after_kv": hook_after_kv, "hook_before_q": hook_before_q})
            build_phaseCD(nc, ioCD)

            with nc.Block() as block:
                @block.gpsimd
                def _(gpsimd):
                    gpsimd.wait_ge(ccg, 32 + 3)

            ioE = {k: io[k] for k in ["maskT", "woT1"]}
            ioE.update({"kg": kg_d.ap(), "vg": vg_d.ap(), "qasg": qasg_d.ap(),
                        "wo_asg": wo_asg_d.ap()})
            build_phaseE(nc, ioE)

            with nc.Block() as block:
                @block.gpsimd
                def _(gpsimd):
                    gpsimd.collective_compute(
                        "AllGather", ALU.bypass, replica_groups=RG,
                        ins=[wo_asg_d.ap()], outs=[wog_d.ap()]).then_inc(ccg, 1)

            def hook_wo_gather(sync):
                sync.wait_ge(ccg, 32 + 4)
                with sync.register("rgF") as rg:
                    for s_ in range(2):
                        sync.reg_load(rg, csb[:1, :1])
                        sync.reg_mul(rg, rg, 524288)
                        sync.reg_add(rg, rg, s_ * 262144)
                        sync.dma_start(
                            out=bass.AP(wo_own_d, s_ * 2097152,
                                        [[262144, 8], [4096, 64], [1, 4096]]),
                            in_=bass.AP(wog_d, rg,
                                        [[4194304, 8], [4096, 64], [1, 4096]])) \
                            .then_inc(ccg, 16)
                sync.wait_ge(ccg, 64 + 4)

            ioF = {k: io[k] for k in ["worgT", "ustack"]}
            ioF.update({"x1": x1_d.ap(), "xn1fm": xn1fm_d.ap(), "hsc_d": hsc_d.ap(),
                        "wo_own": wo_own_d.ap(), "y": io["y"],
                        "hook_wo_gather": hook_wo_gather})
            build_phaseF(nc, ioF)
    return nc


def kernel(**inputs):
    W = {k: np.asarray(v) for k, v in inputs.items()}
    x_full = np.ascontiguousarray(W["L0_attn_hidden_states"][0])
    p = host_prep_shared(W)
    pcs = [host_prep_core(c) for c in range(NCORE)]
    EI, EO = "ExternalInput", "ExternalOutput"
    nc = bass.Bass("TRN2", target_bir_lowering=False, detect_race_conditions=False)
    io = _declare(nc, [
        ("x", [TOK, D], F32, EI), ("cid", [1, 1], mybir.dt.int32, EI),
        ("idmat", [128, 128], F32, EI),
        ("wqselT", [4096, 1024], F32, EI), ("bqsel_pc", [128, 8], F32, EI),
        ("m_pc", [128, 1024], F32, EI), ("z_pc", [128, 8], F32, EI),
        ("rwT", [4096, 64], F32, EI), ("rbias", [1, 64], F32, EI),
        ("eb", [64, 4096], F32, EI), ("woT0", [4096, 4096], F32, EI),
        ("wkrT", [4096, 512], F32, EI), ("wdkT", [4096, 512], F32, EI),
        ("wupkT", [512, 512], F32, EI), ("wupvT", [512, 1024], F32, EI),
        ("wqT1", [4096, 4096], F32, EI), ("wql1T", [4096, 128], F32, EI),
        ("lkT", [128, 8], F32, EI), ("ld2", [4096, 512], F32, EI),
        ("kcosE", [1024, 512], F32, EI), ("ksinE", [1024, 512], F32, EI),
        ("cosq", [128, 1024], F32, EI), ("sinq", [128, 1024], F32, EI),
        ("maskT", [512, 64], F32, EI), ("woT1", [4096, 4096], F32, EI),
        ("worgT", [4096, 4096], F32, EI), ("ustack", [512, 4096], F32, EI),
        ("y", [TOK, D], F32, EO),
    ])
    build_fused(nc, io)
    shared = {k: p[k] for k in ["idmat", "wqselT", "bqsel_pc", "m_pc", "z_pc", "rwT",
                                "rbias", "eb", "woT0", "wkrT", "wdkT", "wupkT", "wupvT",
                                "wqT1", "wql1T", "lkT", "ld2", "woT1", "worgT", "ustack"]}
    in_maps = []
    for c in range(NCORE):
        m = dict(shared)
        m["x"] = np.ascontiguousarray(x_full[TOK * c:TOK * (c + 1)])
        m["cid"] = np.array([[c]], np.int32)
        for k in ["kcosE", "ksinE", "cosq", "sinq", "maskT"]:
            m[k] = pcs[c][k]
        in_maps.append(m)
    res = run_bass_kernel_spmd(nc, in_maps, core_ids=list(range(NCORE)))
    y = np.concatenate([np.asarray(res.results[c]["y"]) for c in range(NCORE)], axis=0)
    return y[None, :, :]


F32_ = np.float32


def _declare(nc, specs):
    io = {}
    for name, shape, dt, kind in specs:
        t = nc.dram_tensor(name, shape, dt, kind=kind) if kind else nc.dram_tensor(name, shape, dt)
        io[name] = t.ap()
    return io


def kernel_unfused(**inputs):
    W = {k: np.asarray(v) for k, v in inputs.items()}
    x_full = np.ascontiguousarray(W["L0_attn_hidden_states"][0])
    p = host_prep_shared(W)
    pcs = [host_prep_core(c) for c in range(NCORE)]
    EI, EO = "ExternalInput", "ExternalOutput"

    ncA = bass.Bass("TRN2", target_bir_lowering=False, detect_race_conditions=False)
    ioA = _declare(ncA, [
        ("x", [TOK, D], F32, EI), ("idmat", [128, 128], F32, EI),
        ("wqselT", [4096, 1024], F32, EI), ("bqsel_pc", [128, 8], F32, EI),
        ("m_pc", [128, 1024], F32, EI), ("z_pc", [128, 8], F32, EI),
        ("rwT", [4096, 64], F32, EI), ("rbias", [1, 64], F32, EI),
        ("eb", [64, 4096], F32, EI), ("woT0", [4096, 4096], F32, EI),
        ("x1", [TOK, D], F32, EO), ("xn1tm", [TOK, D], BF16, EO),
        ("xn1fm", [128, NCH, TOK], BF16, EO), ("partials", [128, NCH], F32, EO),
        ("denb_d", [1, TOK], F32, None),
    ])
    build_phaseA(ncA, ioA)
    shared_A = {k: p[k] for k in ["idmat", "wqselT", "bqsel_pc", "m_pc", "z_pc",
                                  "rwT", "rbias", "eb", "woT0"]}
    in_mapsA = []
    for c in range(NCORE):
        m = dict(shared_A)
        m["x"] = np.ascontiguousarray(x_full[TOK * c:TOK * (c + 1)])
        in_mapsA.append(m)
    resA = run_bass_kernel_spmd(ncA, in_mapsA, core_ids=list(range(NCORE)))
    x1 = [np.asarray(resA.results[c]["x1"]) for c in range(NCORE)]
    xn1fm = [np.asarray(resA.results[c]["xn1fm"]) for c in range(NCORE)]
    partials_all = np.ascontiguousarray(
        np.stack([np.asarray(resA.results[c]["partials"]) for c in range(NCORE)]))
    xn1_full = np.concatenate([np.asarray(resA.results[c]["xn1tm"]) for c in range(NCORE)],
                              axis=0)

    ncB = bass.Bass("TRN2", target_bir_lowering=False, detect_race_conditions=False)
    ioB = _declare(ncB, [
        ("xn1fm", [128, NCH, TOK], BF16, EI), ("partials_all", [8, 128, NCH], F32, EI),
        ("xnasg", [TOK, D], BF16, EI), ("idmat", [128, 128], F32, EI),
        ("wkrT", [4096, 512], F32, EI), ("wdkT", [4096, 512], F32, EI),
        ("wupkT", [512, 512], F32, EI), ("wupvT", [512, 1024], F32, EI),
        ("wqT1", [4096, 4096], F32, EI), ("wql1T", [4096, 128], F32, EI),
        ("lkT", [128, 8], F32, EI), ("ld2", [4096, 512], F32, EI),
        ("kcosE", [1024, 512], F32, EI), ("ksinE", [1024, 512], F32, EI),
        ("cosq", [128, 1024], F32, EI), ("sinq", [128, 1024], F32, EI),
        ("kown", [8, 128, 1024], BF16, EO), ("vout", [8, 1024, 128], BF16, EO),
        ("hsc_d", [128, 4, TOK], BF16, EO), ("qasg_d", [32, 128, 1024], BF16, EO),
        ("sc8_d", [8, 1], F32, None), ("cf8_d", [1, 8], F32, None),
    ])
    build_phaseCD(ncB, ioB)
    shared_B = {k: p[k] for k in ["idmat", "wkrT", "wdkT", "wupkT", "wupvT", "wqT1",
                                  "wql1T", "lkT", "ld2"]}
    in_mapsB = []
    for c in range(NCORE):
        m = dict(shared_B)
        m["xn1fm"] = xn1fm[c]
        m["partials_all"] = partials_all
        m["xnasg"] = np.ascontiguousarray(xn1_full[pcs[c]["pos_asg"]])
        for k in ["kcosE", "ksinE", "cosq", "sinq"]:
            m[k] = pcs[c][k]
        in_mapsB.append(m)
    resB = run_bass_kernel_spmd(ncB, in_mapsB, core_ids=list(range(NCORE)))
    kfull = np.concatenate([np.asarray(resB.results[c]["kown"]) for c in range(NCORE)],
                           axis=2)
    vfull = np.concatenate([np.asarray(resB.results[c]["vout"]) for c in range(NCORE)],
                           axis=1)
    hsc = [np.asarray(resB.results[c]["hsc_d"]) for c in range(NCORE)]
    qasg = [np.asarray(resB.results[c]["qasg_d"]) for c in range(NCORE)]

    ncC = bass.Bass("TRN2", target_bir_lowering=False, detect_race_conditions=False)
    ioC = _declare(ncC, [
        ("kfull", [8, 128, S], BF16, EI), ("vfull", [8, S, 128], BF16, EI),
        ("qasg", [32, 128, 1024], BF16, EI), ("maskT", [512, 64], F32, EI),
        ("woT1", [4096, 4096], F32, EI),
        ("wo_asg", [TOK, D], F32, EO),
    ])
    build_phaseE(ncC, ioC)
    in_mapsC = []
    for c in range(NCORE):
        in_mapsC.append({"kfull": kfull, "vfull": vfull, "qasg": qasg[c],
                         "maskT": pcs[c]["maskT"], "woT1": p["woT1"]})
    resC = run_bass_kernel_spmd(ncC, in_mapsC, core_ids=list(range(NCORE)))
    wo_asg = [np.asarray(resC.results[c]["wo_asg"]) for c in range(NCORE)]

    wo_global = np.empty((S, D), np.float32)
    for g in range(128):
        wo_global[64 * g:64 * (g + 1)] = wo_asg[g % 8][(g // 8) * 64:(g // 8) * 64 + 64]

    ncD = bass.Bass("TRN2", target_bir_lowering=False, detect_race_conditions=False)
    ioD = _declare(ncD, [
        ("x1", [TOK, D], F32, EI), ("xn1fm", [128, NCH, TOK], BF16, EI),
        ("hsc_d", [128, 4, TOK], BF16, EI), ("wo_own", [TOK, D], F32, EI),
        ("worgT", [4096, 4096], F32, EI), ("ustack", [512, 4096], F32, EI),
        ("y", [TOK, D], F32, EO),
    ])
    build_phaseF(ncD, ioD)
    in_mapsD = []
    for c in range(NCORE):
        in_mapsD.append({"x1": x1[c], "xn1fm": xn1fm[c], "hsc_d": hsc[c],
                         "wo_own": np.ascontiguousarray(wo_global[TOK * c:TOK * (c + 1)]),
                         "worgT": p["worgT"], "ustack": p["ustack"]})
    resD = run_bass_kernel_spmd(ncD, in_mapsD, core_ids=list(range(NCORE)))
    y = np.concatenate([np.asarray(resD.results[c]["y"]) for c in range(NCORE)], axis=0)
    return y[None, :, :]
